# version 04
# baseline (speedup 1.0000x reference)
import numpy as np
from contextlib import ExitStack

import concourse.bass as bass
import concourse.mybir as mybir
from concourse.bass_utils import run_bass_kernel_spmd

F32 = mybir.dt.float32
BF16 = mybir.dt.bfloat16
AF = mybir.ActivationFunctionType
ALU = mybir.AluOpType
AX = mybir.AxisListType

NCORE = 8
D = 1024
DC = 8
TC = 2048
S = 64
E = 16
H = 512
HS = 1024
F4 = 4096
CH = 128
NCH = TC // CH
SEG = 128
LN_EPS = 1e-5
ATTN_EPS = 1e-6
BIGNEG = -30000.0

_NSUB = 4
_STAGE = 99


class Sem:
    def __init__(self, es, nc, name):
        self.h = es.enter_context(nc.semaphore(name))
        self.n = 0


class B:
    ENGINES = ("sync", "vector", "scalar", "tensor", "gpsimd")

    def __init__(self, nc):
        self.nc = nc
        self.q = {e: [] for e in self.ENGINES}

    def do(self, eng, fn):
        self.q[eng].append(fn)

    def wait(self, eng, *toks):
        best = {}
        for t in toks:
            if t is None:
                continue
            s, v = t
            if v > 0:
                best[s.h.num] = (s, max(v, best.get(s.h.num, (s, 0))[1]))
        for s, v in best.values():
            self.do(eng, lambda e, s=s, v=v: e.wait_ge(s.h, v))

    def inc(self, eng, fn, sem, n=1):
        sem.n += n
        v = sem.n
        self.do(eng, lambda e, f=fn, s=sem, n=n: f(e).then_inc(s.h, n))
        return (sem, v)

    def dma(self, eng, out, in_, sem):
        return self.inc(eng, lambda e: e.dma_start(out=out, in_=in_), sem, 16)

    def last(self, eng, sem, n=1):
        fn = self.q[eng].pop()
        sem.n += n
        v = sem.n
        self.q[eng].append(lambda e, f=fn, s=sem, n=n: f(e).then_inc(s.h, n))
        return (sem, v)


def build_nc(n_sub=4):
    nc = bass.Bass("TRN2", target_bir_lowering=False)
    es = ExitStack()

    din = lambda name, shape, dt=F32: nc.dram_tensor(name, shape, dt, kind="ExternalInput").ap()
    dint = lambda name, shape, dt=F32: nc.dram_tensor(name, shape, dt, kind="Internal").ap()

    xin = din("xin", [D, TC])
    xsh_in = din("xsh_in", [D, TC])
    a_wshT = din("a_wshT", [D, D])
    a_wkT = din("a_wkT", [D, S])
    a_wvT = din("a_wvT", [D, S])
    a_woT = din("a_woT", [S, D])
    a_gate = din("a_gate", [128, DC])
    a_lnw = din("a_lnw", [128, DC])
    a_lnb = din("a_lnb", [128, DC])
    a_td = din("a_td", [S, 1])
    a_tf = din("a_tf", [S, 1])
    rwkv_base = din("rwkv_base", [S, NCORE])
    rwkv_mask = din("rwkv_mask", [S, NCORE])
    f_gT = din("f_gT", [D, E])
    f_wsin = din("f_wsin", [D, HS])
    f_vsin = din("f_vsin", [D, HS])
    f_wsout = din("f_wsout", [HS, D])
    f_ewin = din("f_ewin", [E * D, H])
    f_evin = din("f_evin", [E * D, H])
    f_ewout = din("f_ewout", [E * H, D])
    b_ln1w = din("b_ln1w", [128, DC]); b_ln1b = din("b_ln1b", [128, DC])
    b_ln2w = din("b_ln2w", [128, DC]); b_ln2b = din("b_ln2b", [128, DC])
    b_qT = din("b_qT", [D, S]); b_qb = din("b_qb", [S, 1])
    b_kT = din("b_kT", [D, S]); b_kb = din("b_kb", [S, 1])
    b_vT = din("b_vT", [D, D]); b_vbrow = din("b_vbrow", [1, D])
    b_oT = din("b_oT", [D, D]); b_ob = din("b_ob", [128, DC])
    b_f1T = din("b_f1T", [D, F4]); b_f1b = din("b_f1b", [128, 32])
    b_f2T = din("b_f2T", [F4, D]); b_f2b = din("b_f2b", [128, DC])
    kv_maskin = din("kv_maskin", [S, NCORE])
    ohc_in = din("ohc_in", [S, NCORE])
    ident_in = din("ident_in", [128, 128])
    identb_in = din("identb_in", [128, 128], BF16)
    tri_in = din("tri_in", [128, 128], BF16)
    ones_f_in = din("ones_f_in", [1, 128])
    ones_b_in = din("ones_b_in", [1, 128], BF16)
    onesc_b_in = din("onesc_b_in", [128, 1], BF16)
    esel_in = din("esel_in", [E, E * 128], BF16)
    consts_in = din("consts_in", [128, 8])

    xout = nc.dram_tensor("xout", [D, TC], F32, kind="ExternalOutput").ap()


    h_sc = dint("h_sc", [E * H + HS, TC], BF16)
    wout_sc = dint("wout_sc", [E * H + HS, D], BF16)
    vtm_sc = dint("vtm_sc", [TC, D], BF16)
    kvpref_sc = dint("kvpref_sc", [NCH * S, D], BF16)
    f2_sc = dint("f2_sc", [F4, D], BF16)
    h1_sc = dint("h1_sc", [F4, TC], BF16)
    z_sc = dint("z_sc", [D, TC], BF16)
    ag1_in = dint("ag1_in", [S, NCORE])
    ag1_out = dint("ag1_out", [S, NCORE])
    ag2_in = dint("ag2_in", [S, NCORE * 1040])
    ag2_out = dint("ag2_out", [S, NCORE * 1040])

    sb = lambda n, s, d=F32: es.enter_context(nc.sbuf_tensor(n, s, d))
    with es:
        x = sb("x", [128, DC * TC])
        B1 = sb("B1", [128, DC * TC], BF16)
        B2 = sb("B2", [128, 12288], BF16)
        SV = sb("SV", [128, 4096], BF16)
        SC = sb("SC", [128, 2048])
        SCB = sb("SCB", [128, 6144], BF16)
        PBK = sb("PBK", [S, 2 * TC], BF16)
        wkvb = PBK[0:S, 0:TC]
        states = PBK[0:S, TC:2 * TC]
        ksqk = sb("ksqk", [S, TC], BF16)
        qb16 = PBK[0:S, 0:TC]
        kb16 = PBK[0:S, TC:2 * TC]
        kvpref_b = sb("kvpref_b", [S, D], BF16)
        kvin_b = sb("kvin_b", [S, D + 16], BF16)
        kvrun = sb("kvrun", [S, D + 16])
        agkv = sb("agkv", [S, 528])
        esel = sb("esel", [E, E * 128], BF16)
        combfm_b = sb("combfm_b", [E, TC], BF16)
        gcomb = sb("gcomb", [128, 3 * E * NCH])
        stage = sb("stage", [128, 1024], BF16)
        cbe = SCB[:, 4096:5120]
        hstr = stage[:, 0:1024]
        ident = sb("ident", [128, 128])
        identb = sb("identb", [128, 128], BF16)
        tri = sb("tri", [128, 128], BF16)
        ones_f = sb("ones_f", [1, 128])
        ones_b = sb("ones_b", [1, 128], BF16)
        onesc_b = sb("onesc_b", [128, 1], BF16)
        vec8 = sb("vec8", [128, 80])
        v64 = sb("v64", [S, 64])
        tr64 = sb("tr64", [S, 16])
        m128 = sb("m128", [128, 48])
        f1b_sb = sb("f1b_sb", [128, 32])
        vbrow_b = sb("vbrow_b", [1, D], BF16)
        denloc = sb("denloc", [128, NCH])
        den1 = sb("den1", [128, 4])
        qpb = sb("qpb", [S, CH], BF16)
        ktm_b = sb("ktm_b", [CH, S], BF16)
        atile = sb("atile", [128, CH], BF16)
        rrow = sb("rrow", [1, 2 * CH])
        wful_t = sb("wful_t", [S, 2 * SEG])
        ag1f = sb("ag1f", [S, NCORE])
        consts = sb("consts", [128, 8])
        _cvals = [LN_EPS, ATTN_EPS, -0.5, -1.0, 1.0, 0.0, 0.5, 2.0]
        for _i, _v in enumerate(_cvals):
            nc.const_aps.register(F32, _v, consts[:, _i:_i + 1])

        PS = es.enter_context(nc.psum_tensor("PS", [128, 4096], F32))
        bank = lambda i, n=512: PS[:, i * 512:i * 512 + n]

        R = sb("R", [1, 2048])
        MU = R[0:1, 0:512]
        VAR = R[0:1, 512:1024]
        TROW = R[0:1, 1024:1536]
        ROW2 = R[0:1, 1536:2048]
        SBP = sb("SBP", [128, 2048], BF16)
        M2BP = sb("M2BP", [128, 2048], BF16)

        SB_ = SCB[:, 0:2048]
        M2B = SCB[:, 2048:4096]
        SQR = SCB[:, 4096:6144]

        sW = Sem(es, nc, "sW")
        sX = Sem(es, nc, "sX")
        sW2 = Sem(es, nc, "sW2")
        sQ = Sem(es, nc, "sQ")
        sStA = Sem(es, nc, "sStA")
        sStB = Sem(es, nc, "sStB")
        sG = Sem(es, nc, "sG")
        sGC = Sem(es, nc, "sGC")
        sV = Sem(es, nc, "sV")
        sA = Sem(es, nc, "sA")
        sT = Sem(es, nc, "sT")
        sC1 = Sem(es, nc, "sC1")
        sC2 = Sem(es, nc, "sC2")

        b = B(nc)

        def finish(nc_, es_, b_, tchain, **kw):
            b.wait("sync", tchain)
            sO = Sem(es, nc, "sO")
            tso = b.dma("sync", xout.rearrange("(dc p) t -> p dc t", p=128),
                        x[:].rearrange("p (dc t) -> p dc t", t=TC), sO)
            b.wait("sync", tso)
            b.wait("gpsimd", tso)
            b.wait("vector", tso)
            b.wait("scalar", tso)
            b.wait("tensor", tso)
            rem = nc.sbuf_bytes_remaining
            assert rem >= 0, f"SBUF over budget by {-rem}"
            blk = es.enter_context(nc.Block())
            blk.sync(lambda eng: [fn(eng) for fn in b.q["sync"]])
            blk.vector(lambda eng: [fn(eng) for fn in b.q["vector"]])
            blk.scalar(lambda eng: [fn(eng) for fn in b.q["scalar"]])
            blk.tensor(lambda eng: [fn(eng) for fn in b.q["tensor"]])
            blk.gpsimd(lambda eng: [fn(eng) for fn in b.q["gpsimd"]])
            es.close()
            return nc

        xv = lambda dc, t0, n: x[:, dc * TC + t0: dc * TC + t0 + n]
        b1v = lambda dc, t0, n: B1[:, dc * TC + t0: dc * TC + t0 + n]

        def emit_rows_bcast(tstat, width, sb_dst=None, m2_dst=None, post_rows=None):
            sb_dst = sb_dst if sb_dst is not None else SB_
            m2_dst = m2_dst if m2_dst is not None else M2B
            b.wait("vector", tstat)
            tb = None
            for tq in range(width // 512):
                s0 = tq * 512
                b.do("vector", lambda e, s0=s0: e.tensor_scalar(
                    out=MU, in0=PS[0:1, s0:s0 + 512], scalar1=1.0 / D, scalar2=None,
                    op0=ALU.mult))
                t_ = b.inc("vector", lambda e, s0=s0: e.tensor_scalar(
                    out=VAR, in0=PS[0:1, 2048 + s0:2048 + s0 + 512],
                    scalar1=1.0 / D, scalar2=None, op0=ALU.mult), sV, 1)
                b.wait("vector", t_)
                t_ = b.inc("vector", lambda e: e.tensor_tensor(out=TROW, in0=MU, in1=MU,
                                                               op=ALU.mult), sV, 1)
                b.wait("vector", t_)
                tv = b.inc("vector", lambda e: e.tensor_tensor(out=VAR, in0=VAR, in1=TROW,
                                                               op=ALU.subtract), sV, 1)
                b.wait("scalar", tv)
                tl_ = b.inc("scalar", lambda e: e.activation(out=VAR, in_=VAR, func=AF.Ln,
                                                             bias=LN_EPS), sA, 1)
                b.wait("scalar", tl_)
                tr = b.inc("scalar", lambda e: e.activation(out=VAR, in_=VAR, func=AF.Exp,
                                                            scale=-0.5), sA, 1)
                b.wait("vector", tr)
                tm2 = b.inc("vector", lambda e: e.tensor_tensor(out=TROW, in0=MU, in1=VAR,
                                                                op=ALU.mult), sV, 1)
                b.wait("vector", tm2)
                srow, m2row, textra = VAR, TROW, tm2
                if post_rows is not None:
                    srow, m2row, textra = post_rows(tq, tm2)
                b.wait("tensor", textra)
                b.do("tensor", lambda e, s0=s0, srow=srow: e.matmul(
                    out=PS[:, s0:s0 + 512], lhsT=ones_f[:], rhs=srow,
                    start=True, stop=True))
                b.do("tensor", lambda e, s0=s0, m2row=m2row: e.matmul(
                    out=PS[:, 2048 + s0:2048 + s0 + 512], lhsT=ones_f[:], rhs=m2row,
                    start=True, stop=True))
                tbm = b.last("tensor", sT)
                b.wait("vector", tbm)
                b.do("vector", lambda e, s0=s0: e.tensor_copy(
                    out=sb_dst[:, s0:s0 + 512], in_=PS[:, s0:s0 + 512]))
                tb = b.inc("vector", lambda e, s0=s0: e.tensor_copy(
                    out=m2_dst[:, s0:s0 + 512], in_=PS[:, 2048 + s0:2048 + s0 + 512]),
                    sV, 1)
                b.wait("tensor", tb)
            return tb

        def emit_ln_stats(src_bf_chunk, after=()):
            b.wait("scalar", *after)
            b.wait("tensor", *after)
            for dc in range(DC):
                tsq = b.inc("scalar", lambda e, dc=dc: e.activation(
                    out=SQR, in_=src_bf_chunk(dc), func=AF.Square), sA, 1)
                for tq in range(4):
                    b.do("tensor", lambda e, dc=dc, tq=tq: e.matmul(
                        out=PS[0:1, tq * 512:(tq + 1) * 512], lhsT=onesc_b[:],
                        rhs=src_bf_chunk(dc)[:, tq * 512:(tq + 1) * 512],
                        start=(dc == 0), stop=(dc == DC - 1)))
                b.wait("tensor", tsq)
                for tq in range(4):
                    b.do("tensor", lambda e, tq=tq, dc=dc: e.matmul(
                        out=PS[0:1, 2048 + tq * 512:2048 + (tq + 1) * 512], lhsT=onesc_b[:],
                        rhs=SQR[:, tq * 512:(tq + 1) * 512],
                        start=(dc == 0), stop=(dc == DC - 1)))
                tmm = b.last("tensor", sT)
                if dc < DC - 1:
                    b.wait("scalar", tmm)
            return emit_rows_bcast(tmm, 2048)

        def emit_norm(src_chunk, dst_chunk, lnw=None, lnwoff=0, after=()):
            b.wait("vector", *after)
            for dc in range(DC):
                b.do("vector", lambda e, dc=dc: e.tensor_tensor(
                    out=dst_chunk(dc), in0=src_chunk(dc), in1=SB_, op=ALU.mult))
            t1 = b.last("vector", sV)
            b.wait("vector", t1)
            for dc in range(DC):
                b.do("vector", lambda e, dc=dc: e.tensor_tensor(
                    out=dst_chunk(dc), in0=dst_chunk(dc), in1=M2B, op=ALU.subtract))
            tn = b.last("vector", sV)
            if lnw is not None:
                b.wait("vector", tn)
                for dc in range(DC):
                    b.do("vector", lambda e, dc=dc: e.tensor_scalar(
                        out=dst_chunk(dc), in0=dst_chunk(dc),
                        scalar1=vec8[:, lnwoff + dc:lnwoff + dc + 1],
                        scalar2=vec8[:, lnwoff + DC + dc:lnwoff + DC + dc + 1],
                        op0=ALU.mult, op1=ALU.add))
                tn = b.last("vector", sV)
            return tn

        def emit_scan(out_fn, d0_ap, d1_fn, n, extra_wait=None, init0=0.0):
            b.wait("vector", extra_wait)
            last = None
            for s0 in range(0, n, SEG):
                init = init0 if s0 == 0 else out_fn(s0 - 1, 1)
                last = b.inc("vector", lambda e, s0=s0, init=init: e.tensor_tensor_scan(
                    out=out_fn(s0, SEG), data0=d0_ap, data1=d1_fn(s0, SEG),
                    initial=init, op0=ALU.mult, op1=ALU.add), sV, 1)
                b.wait("vector", last)
            return last

        for ap_s, ap_d in ((ident, ident_in), (identb, identb_in), (tri, tri_in),
                           (ones_f, ones_f_in), (ones_b, ones_b_in),
                           (onesc_b, onesc_b_in), (esel, esel_in), (f1b_sb, b_f1b),
                           (consts, consts_in)):
            b.dma("sync", ap_s[:], ap_d, sW)
        VEC = {}
        voff = 0
        for name, ap_d in (("a_gate", a_gate), ("a_lnw", a_lnw), ("a_lnb", a_lnb),
                           ("b_ln1w", b_ln1w), ("b_ln1b", b_ln1b),
                           ("b_ln2w", b_ln2w), ("b_ln2b", b_ln2b),
                           ("b_ob", b_ob), ("b_f2b", b_f2b)):
            b.dma("sync", vec8[:, voff:voff + DC], ap_d, sW)
            VEC[name] = voff
            voff += DC
        V64 = {}
        v64off = 0
        for name, ap_d, w in (("a_td", a_td, 1), ("a_tf", a_tf, 1),
                              ("rwkv_base", rwkv_base, NCORE),
                              ("rwkv_mask", rwkv_mask, NCORE),
                              ("kv_mask", kv_maskin, NCORE),
                              ("ohc", ohc_in, NCORE),
                              ("b_qb", b_qb, 1), ("b_kb", b_kb, 1)):
            b.dma("sync", v64[:, v64off:v64off + w], ap_d, sW)
            V64[name] = v64off
            v64off += w
        v64c = lambda nm: v64[:, V64[nm]:V64[nm] + 1]
        tsm = (sW, sW.n)
        tx = b.dma("sync", x[:].rearrange("p (dc t) -> p dc t", t=TC),
                   xin.rearrange("(dc p) t -> p dc t", p=128), sX)

        b.wait("scalar", tsm)
        tg = b.inc("scalar", lambda e: e.activation(
            out=m128[:, 0:DC], in_=vec8[:, VEC["a_gate"]:VEC["a_gate"] + DC],
            func=AF.Sigmoid), sA, 1)
        b.wait("vector", tg)
        tgn = b.inc("vector", lambda e: e.tensor_scalar(
            out=m128[:, 8:16], in0=m128[:, 0:DC], scalar1=-1.0, scalar2=None,
            op0=ALU.mult), sV, 1)

        wsh_r = a_wshT.rearrange("(dk p) m -> p dk m", p=128)
        tcast = []
        for pc in range(8):
            half = SC[:, (pc % 2) * 1024:(pc % 2) * 1024 + 1024]
            if pc >= 2:
                b.wait("sync", tcast[pc - 2])
            tld = b.dma("sync", half, wsh_r[:, pc:pc + 1, :], sStA if pc % 2 == 0 else sStB)
            b.wait("vector", tld)
            tcast.append(b.inc("vector", lambda e, half=half, pc=pc: e.tensor_copy(
                out=B2[:, pc * 1024:(pc + 1) * 1024], in_=half), sV, 1))
        twsh = tcast[-1]
        wsh_t = lambda dk, mc: B2[:, dk * 1024 + mc * 128: dk * 1024 + (mc + 1) * 128]
        b.wait("sync", tcast[-2], tcast[-1])
        t1 = b.dma("sync", SC[:, 0:512].rearrange("p (dk m) -> p dk m", m=S),
                   a_wkT.rearrange("(dk p) m -> p dk m", p=128), sW2)
        t2 = b.dma("sync", SC[:, 512:1024].rearrange("p (dk m) -> p dk m", m=S),
                   a_wvT.rearrange("(dk p) m -> p dk m", p=128), sW2)
        t3 = b.dma("sync", SC[0:S, 1024:2048], a_woT, sW2)
        b.wait("vector", t1, t2, t3)
        b.do("vector", lambda e: e.tensor_copy(out=SV[:, 0:512], in_=SC[:, 0:512]))
        b.do("vector", lambda e: e.tensor_copy(out=SV[:, 512:1024], in_=SC[:, 512:1024]))
        tkvw = b.inc("vector", lambda e: e.tensor_copy(out=SV[0:S, 1024:2048],
                                                       in_=SC[0:S, 1024:2048]), sV, 1)
        wk_t = lambda dk: SV[:, dk * S:(dk + 1) * S]
        wv_t = lambda dk: SV[:, 512 + dk * S: 512 + (dk + 1) * S]
        wo_t = lambda mc: SV[0:S, 1024 + mc * 128: 1024 + (mc + 1) * 128]

        def xroll(dc):
            return stage[:, 0:2048]

        def emit_ln_stats_rolled(src_f32_chunk, after=(), sb_dst=None, m2_dst=None,
                                 post_rows=None):
            tprev = None
            b.wait("vector", *after)
            b.wait("tensor", *after)
            for dc in range(DC):
                if tprev is not None:
                    b.wait("vector", tprev)
                tc_ = b.inc("vector", lambda e, dc=dc: e.tensor_copy(
                    out=SCB[:, 0:2048], in_=src_f32_chunk(dc)), sV, 1)
                b.wait("scalar", tc_)
                tsq = b.inc("scalar", lambda e: e.activation(
                    out=SQR, in_=SCB[:, 0:2048], func=AF.Square), sA, 1)
                b.wait("tensor", tc_)
                for tq in range(4):
                    b.do("tensor", lambda e, tq=tq, dc=dc: e.matmul(
                        out=PS[0:1, tq * 512:(tq + 1) * 512], lhsT=onesc_b[:],
                        rhs=SCB[:, tq * 512:(tq + 1) * 512],
                        start=(dc == 0), stop=(dc == DC - 1)))
                b.wait("tensor", tsq)
                for tq in range(4):
                    b.do("tensor", lambda e, tq=tq, dc=dc: e.matmul(
                        out=PS[0:1, 2048 + tq * 512:2048 + (tq + 1) * 512], lhsT=onesc_b[:],
                        rhs=SQR[:, tq * 512:(tq + 1) * 512],
                        start=(dc == 0), stop=(dc == DC - 1)))
                tprev = b.last("tensor", sT)
                b.wait("scalar", tprev)
            return emit_rows_bcast(tprev, 2048, sb_dst=sb_dst, m2_dst=m2_dst,
                                   post_rows=post_rows)

        if n_sub == 0 or _STAGE < 1:
            return finish(nc, es, b, tx)
        tb1 = emit_ln_stats_rolled(lambda dc: xv(dc, 0, TC), after=[tx, tsm])
        if _STAGE < 1.5:
            return finish(nc, es, b, tb1)
        b.wait("vector", tb1, tgn)
        for dc in range(DC):
            b.do("vector", lambda e, dc=dc: e.tensor_tensor(
                out=b1v(dc, 0, TC), in0=xv(dc, 0, TC), in1=SB_, op=ALU.mult))
        t_ = b.last("vector", sV); b.wait("vector", t_)
        for dc in range(DC):
            b.do("vector", lambda e, dc=dc: e.tensor_tensor(
                out=b1v(dc, 0, TC), in0=b1v(dc, 0, TC), in1=M2B, op=ALU.subtract))
        t_ = b.last("vector", sV); b.wait("vector", t_)
        for dc in range(DC):
            b.do("vector", lambda e, dc=dc: e.scalar_tensor_tensor(
                out=b1v(dc, 0, TC), in0=b1v(dc, 0, TC), scalar=m128[:, 8 + dc:9 + dc],
                in1=b1v(dc, 0, TC), op0=ALU.mult, op1=ALU.add))
        ty1g = b.last("vector", sV)

        if _STAGE < 2:
            return finish(nc, es, b, ty1g)
        def xshq_dram(tq, grp):
            return xsh_in.rearrange("(dc p) t -> p dc t", p=128)[
                :, grp * 4:(grp + 1) * 4, tq * 512:(tq + 1) * 512]

        scq4 = lambda dc4: SC[:, dc4 * 512:(dc4 + 1) * 512]
        yshg = lambda dc4: SV[:, 2048 + dc4 * 512:2048 + (dc4 + 1) * 512]
        tevac_q = None

        def qstats(grp, first, last_tok):
            tprev = last_tok
            for dc4 in range(4):
                dc = grp * 4 + dc4
                if tprev is not None:
                    b.wait("vector", tprev)
                tc_ = b.inc("vector", lambda e, dc4=dc4: e.tensor_copy(
                    out=stage[:, 0:512], in_=scq4(dc4)), sV, 1)
                b.wait("scalar", tc_)
                tsq = b.inc("scalar", lambda e: e.activation(
                    out=SQR[:, 0:512], in_=stage[:, 0:512], func=AF.Square), sA, 1)
                b.wait("tensor", tc_)
                b.do("tensor", lambda e, dc=dc, first=first: e.matmul(
                    out=PS[0:1, 0:512], lhsT=onesc_b[:], rhs=stage[:, 0:512],
                    start=(first and dc % 4 == 0 and dc < 4) or (dc == 0),
                    stop=(dc == DC - 1)))
                b.wait("tensor", tsq)
                b.do("tensor", lambda e, dc=dc: e.matmul(
                    out=PS[0:1, 512:1024], lhsT=onesc_b[:], rhs=SQR[:, 0:512],
                    start=(dc == 0), stop=(dc == DC - 1)))
                tprev = b.last("tensor", sT)
                b.wait("scalar", tprev)
            return tprev

        for tq in range(4):
            if tevac_q is not None:
                b.wait("sync", tevac_q)
            else:
                b.wait("sync", tkvw)
            tld = b.dma("sync", SC[:, 0:2048].rearrange("p (c t) -> p c t", t=512),
                        xshq_dram(tq, 0), sQ)
            b.wait("vector", tld)
            b.wait("tensor", tld)
            t_s0 = qstats(0, True, None)
            b.wait("sync", t_s0)
            tld1 = b.dma("sync", SC[:, 0:2048].rearrange("p (c t) -> p c t", t=512),
                         xshq_dram(tq, 1), sQ)
            b.wait("vector", tld1)
            b.wait("tensor", tld1)
            t_s1 = qstats(1, False, tld1)
            b.wait("vector", t_s1)
            b.do("vector", lambda e: e.tensor_scalar(out=MU, in0=PS[0:1, 0:512],
                                                     scalar1=1.0 / D, scalar2=None,
                                                     op0=ALU.mult))
            t_ = b.inc("vector", lambda e: e.tensor_scalar(out=VAR, in0=PS[0:1, 512:1024],
                                                           scalar1=1.0 / D, scalar2=None,
                                                           op0=ALU.mult), sV, 1)
            b.wait("vector", t_)
            t_ = b.inc("vector", lambda e: e.tensor_tensor(out=TROW, in0=MU, in1=MU,
                                                           op=ALU.mult), sV, 1)
            b.wait("vector", t_)
            tv = b.inc("vector", lambda e: e.tensor_tensor(out=VAR, in0=VAR, in1=TROW,
                                                           op=ALU.subtract), sV, 1)
            b.wait("scalar", tv)
            tl_ = b.inc("scalar", lambda e: e.activation(out=VAR, in_=VAR, func=AF.Ln,
                                                         bias=LN_EPS), sA, 1)
            b.wait("scalar", tl_)
            trq = b.inc("scalar", lambda e: e.activation(out=VAR, in_=VAR, func=AF.Exp,
                                                         scale=-0.5), sA, 1)
            b.wait("vector", trq)
            tm2q = b.inc("vector", lambda e: e.tensor_tensor(out=TROW, in0=MU, in1=VAR,
                                                             op=ALU.mult), sV, 1)
            b.wait("vector", tm2q)
            b.wait("tensor", tm2q)
            b.do("tensor", lambda e: e.matmul(out=PS[:, 1024:1536], lhsT=ones_f[:],
                                              rhs=VAR, start=True, stop=True))
            b.do("tensor", lambda e: e.matmul(out=PS[:, 1536:2048], lhsT=ones_f[:],
                                              rhs=TROW, start=True, stop=True))
            tbq = b.last("tensor", sT)
            b.wait("vector", tbq)
            b.do("vector", lambda e: e.tensor_copy(out=SB_[:, 0:512], in_=PS[:, 1024:1536]))
            tbc = b.inc("vector", lambda e: e.tensor_copy(out=M2B[:, 0:512],
                                                          in_=PS[:, 1536:2048]), sV, 1)
            b.wait("vector", tbc)
            for dc4 in range(4):
                b.do("vector", lambda e, dc4=dc4: e.tensor_tensor(
                    out=yshg(dc4), in0=scq4(dc4), in1=SB_[:, 0:512], op=ALU.mult))
            t_ = b.last("vector", sV); b.wait("vector", t_)
            for dc4 in range(4):
                b.do("vector", lambda e, dc4=dc4: e.tensor_tensor(
                    out=yshg(dc4), in0=yshg(dc4), in1=M2B[:, 0:512], op=ALU.subtract))
            tn1 = b.last("vector", sV)
            b.wait("tensor", tn1)
            if tq == 0:
                b.wait("tensor", twsh, ty1g)
            for dc in range(DC):
                for dk4 in range(4):
                    b.do("tensor", lambda e, dc=dc, dk4=dk4: e.matmul(
                        out=bank(dc), lhsT=wsh_t(4 + dk4, dc), rhs=yshg(dk4),
                        start=(dk4 == 0), stop=False))
            tmm1 = b.last("tensor", sT)
            b.wait("sync", tn1)
            tld0 = b.dma("sync", SC[:, 0:2048].rearrange("p (c t) -> p c t", t=512),
                         xshq_dram(tq, 0), sQ)
            b.wait("vector", tld0, tmm1)
            for dc4 in range(4):
                b.do("vector", lambda e, dc4=dc4: e.tensor_tensor(
                    out=yshg(dc4), in0=scq4(dc4), in1=SB_[:, 0:512], op=ALU.mult))
            t_ = b.last("vector", sV); b.wait("vector", t_)
            for dc4 in range(4):
                b.do("vector", lambda e, dc4=dc4: e.tensor_tensor(
                    out=yshg(dc4), in0=yshg(dc4), in1=M2B[:, 0:512], op=ALU.subtract))
            tn0 = b.last("vector", sV)
            b.wait("tensor", tn0)
            for dc in range(DC):
                for dk4 in range(4):
                    b.do("tensor", lambda e, dc=dc, dk4=dk4: e.matmul(
                        out=bank(dc), lhsT=wsh_t(dk4, dc), rhs=yshg(dk4),
                        start=False, stop=(dk4 == 3)))
            tmm = b.last("tensor", sT)
            b.wait("vector", tmm)
            for dc in range(DC):
                b.do("vector", lambda e, tq=tq, dc=dc: e.scalar_tensor_tensor(
                    out=b1v(dc, tq * 512, 512), in0=bank(dc), scalar=m128[:, dc:dc + 1],
                    in1=b1v(dc, tq * 512, 512), op0=ALU.mult, op1=ALU.add))
            tevac_q = b.last("vector", sV)
            b.wait("tensor", tevac_q)
        txshift = tevac_q

        if _STAGE < 3:
            return finish(nc, es, b, txshift)
        tb2 = emit_ln_stats(lambda dc: b1v(dc, 0, TC), after=[txshift])
        txn = emit_norm(lambda dc: b1v(dc, 0, TC), lambda dc: b1v(dc, 0, TC),
                        lnw=True, lnwoff=VEC["a_lnw"], after=[tb2])

        if _STAGE < 4:
            return finish(nc, es, b, txn)
        b.wait("tensor", txn, tkvw)
        for tq in range(4):
            for dk in range(DC):
                b.do("tensor", lambda e, tq=tq, dk=dk: e.matmul(
                    out=PS[0:S, tq * 512:(tq + 1) * 512], lhsT=wk_t(dk),
                    rhs=b1v(dk, tq * 512, 512), start=(dk == 0), stop=(dk == DC - 1)))
        for tq in range(4):
            for dk in range(DC):
                b.do("tensor", lambda e, tq=tq, dk=dk: e.matmul(
                    out=PS[0:S, 2048 + tq * 512:2048 + (tq + 1) * 512], lhsT=wv_t(dk),
                    rhs=b1v(dk, tq * 512, 512), start=(dk == 0), stop=(dk == DC - 1)))
        tkv = b.last("tensor", sT)
        b.wait("scalar", tkv)
        b.do("scalar", lambda e: e.activation(out=v64[:, 62:63], in_=v64c("a_tf"),
                                              func=AF.Exp))
        tsg = b.inc("scalar", lambda e: e.activation(out=SC[0:S, 0:2048],
                                                     in_=PS[0:S, 0:2048],
                                                     func=AF.Sigmoid), sA, 1)
        b.wait("vector", tsg)
        tt1 = b.inc("vector", lambda e: e.tensor_scalar(
            out=SC[0:S, 0:2048], in0=SC[0:S, 0:2048], scalar1=v64[:, 62:63],
            scalar2=None, op0=ALU.mult), sV, 1)
        b.wait("scalar", tt1)
        te1 = b.inc("scalar", lambda e: e.activation(out=SC[0:S, 0:2048],
                                                     in_=SC[0:S, 0:2048], func=AF.Exp,
                                                     scale=-1.0), sA, 1)
        b.wait("vector", te1)
        twkv = b.inc("vector", lambda e: e.tensor_tensor(
            out=PBK[0:S, 0:2048], in0=SC[0:S, 0:2048], in1=PS[0:S, 2048:4096],
            op=ALU.mult), sV, 1)

        b.do("scalar", lambda e: e.activation(out=v64[:, 61:62], in_=v64c("a_td"),
                                              func=AF.Exp))
        tdw = b.last("scalar", sA)
        wful = wful_t[0:S, 0:SEG]
        zful = wful_t[0:S, SEG:2 * SEG]
        b.wait("vector", tdw, twkv)
        b.do("vector", lambda e: e.memset(wful, 1.0))
        t_ = b.inc("vector", lambda e: e.memset(zful, 0.0), sV, 1)
        b.wait("vector", t_)
        twf = b.inc("vector", lambda e: e.tensor_scalar(out=wful, in0=wful,
                                                        scalar1=v64[:, 61:62],
                                                        scalar2=None, op0=ALU.mult), sV, 1)
        b.wait("vector", twf)
        tscan = emit_scan(lambda c0, n: states[:, c0:c0 + n], wful,
                          lambda c0, n: wkvb[:, c0:c0 + n], TC)

        if _STAGE < 5:
            return finish(nc, es, b, tscan)
        b.wait("vector", tscan)
        tf32 = b.inc("vector", lambda e: e.tensor_copy(out=tr64[:, 2:3],
                                                       in_=states[:, TC - 1:TC]), sV, 1)
        b.wait("vector", tf32)
        tarin = b.inc("vector", lambda e: e.tensor_scalar(
            out=ag1f[0:S, 0:NCORE], in0=v64[:, V64["ohc"]:V64["ohc"] + NCORE],
            scalar1=tr64[:, 2:3], scalar2=None, op0=ALU.mult), sV, 1)
        b.wait("gpsimd", tarin)
        tfd = b.dma("gpsimd", ag1_in, ag1f[0:S, 0:NCORE], sG)
        b.wait("gpsimd", tfd)
        tcc1 = b.inc("gpsimd", lambda e: e.collective_compute(
            "AllReduce", ALU.add, replica_groups=[list(range(NCORE))],
            ins=[ag1_in], outs=[ag1_out]), sC1, 1)
        b.wait("gpsimd", tcc1)
        tags = b.dma("gpsimd", tr64[:, 0:8], ag1_out, sG)
        b.wait("vector", tags)
        t_ = (sV, sV.n)
        tc1 = b.inc("vector", lambda e: e.tensor_scalar(
            out=tr64[:, 8:16], in0=v64[:, V64["rwkv_base"]:V64["rwkv_base"] + NCORE],
            scalar1=v64c("a_td"), scalar2=None, op0=ALU.mult), sV, 1)
        b.wait("scalar", tc1)
        tc2 = b.inc("scalar", lambda e: e.activation(out=tr64[:, 8:16], in_=tr64[:, 8:16],
                                                     func=AF.Exp), sA, 1)
        b.wait("vector", tc2, t_)
        t_ = b.inc("vector", lambda e: e.tensor_tensor(
            out=tr64[:, 8:16], in0=tr64[:, 8:16],
            in1=v64[:, V64["rwkv_mask"]:V64["rwkv_mask"] + NCORE], op=ALU.mult), sV, 1)
        b.wait("vector", t_)
        t_ = b.inc("vector", lambda e: e.tensor_tensor(out=tr64[:, 8:16], in0=tr64[:, 8:16],
                                                       in1=tr64[:, 0:8], op=ALU.mult), sV, 1)
        b.wait("vector", t_)
        tsin = b.inc("vector", lambda e: e.tensor_reduce(
            out=tr64[:, 1:2], in_=tr64[:, 8:16], axis=AX.X, op=ALU.add), sV, 1)
        b.wait("vector", tsin)
        twp = emit_scan(lambda c0, n: SC[0:S, c0:c0 + n], wful,
                        lambda c0, n: zful[:, 0:n], TC, init0=1.0)
        b.wait("vector", twp)
        tst = b.inc("vector", lambda e: e.scalar_tensor_tensor(
            out=states[:], in0=SC[0:S, 0:2048], scalar=tr64[:, 1:2], in1=states[:],
            op0=ALU.mult, op1=ALU.add), sV, 1)
        states_b = states

        if _STAGE < 6:
            return finish(nc, es, b, tst)
        b.wait("tensor", tst)
        tev = None
        for tq in range(4):
            for dc in range(DC):
                b.do("tensor", lambda e, tq=tq, dc=dc: e.matmul(
                    out=bank(dc), lhsT=wo_t(dc), rhs=states_b[:, tq * 512:(tq + 1) * 512],
                    start=True, stop=True))
            tmo = b.last("tensor", sT)
            b.wait("vector", tmo)
            for dc in range(DC):
                b.do("vector", lambda e, tq=tq, dc=dc: e.tensor_tensor(
                    out=xv(dc, tq * 512, 512), in0=bank(dc), in1=xv(dc, tq * 512, 512),
                    op=ALU.add))
            tev = b.last("vector", sV)
            b.wait("tensor", tev)
        ts1 = tev


        sH = Sem(es, nc, "sH")
        sHo = [Sem(es, nc, f"sHo{i}") for i in range(2)]
        sWoD = Sem(es, nc, "sWoD")
        sHsl = [Sem(es, nc, f"sHsl{i}") for i in range(8)]
        sWA = Sem(es, nc, "sWA")
        sWB = Sem(es, nc, "sWB")
        NU = E + 2
        WSL = [[0, 1, 2], [2, 0, 1], [1, 2, 0]]
        b2s = lambda sl, c0, n: B2[:, sl * 4096 + c0: sl * 4096 + c0 + n]
        VCX = SCB[:, 5248:5760]
        WOC = SCB[:, 5760:6016]

        def unit_dram(u):
            if u < E:
                return (f_ewin[u * D:(u + 1) * D, :], f_evin[u * D:(u + 1) * D, :],
                        f_ewout[u * H:(u + 1) * H, :], True)
            sub = u - E
            return (f_wsin[:, sub * H:(sub + 1) * H], f_vsin[:, sub * H:(sub + 1) * H],
                    f_wsout[sub * H:(sub + 1) * H, :], False)

        st_state = {"i": 0, "cons": {0: None, 1: None}}

        def stage_load(dram_piece, dst_bf, tgate=None):
            i = st_state["i"] % 2
            st_state["i"] += 1
            free = 1
            for dsz in dram_piece.shape[1:]:
                free *= dsz
            slot = SC[0:dram_piece.shape[0], i * 1024:i * 1024 + free]
            slot_in = slot
            if len(dram_piece.shape) == 3:
                slot_in = slot.rearrange("p (a m) -> p a m", m=dram_piece.shape[-1])
            if st_state["cons"][i] is not None:
                b.wait("sync", st_state["cons"][i])
            if tgate is not None:
                b.wait("sync", tgate)
            tld = b.dma("sync", slot_in, dram_piece, sStA if i == 0 else sStB)
            b.wait("gpsimd", tld)
            tc = b.inc("gpsimd", lambda e, slot=slot, dst_bf=dst_bf: e.tensor_copy(
                out=dst_bf, in_=slot), sGC, 1)
            st_state["cons"][i] = tc
            return tc

        def load_unit_w(u, which, slot, tgate=None):
            wdram = unit_dram(u)[which]
            w3 = wdram.rearrange("(dk p) m -> p dk m", p=128)
            toks = []
            for pc in range(4):
                toks.append(stage_load(w3[:, 2 * pc:2 * pc + 2, :],
                                       b2s(slot, pc * 1024, 1024), tgate))
            return toks[-1]

        def emit_ffn(tprev, first_call):
            tb = emit_ln_stats_rolled(lambda dc: xv(dc, 0, TC), after=[tprev])
            txn = emit_norm(lambda dc: xv(dc, 0, TC), lambda dc: b1v(dc, 0, TC),
                            after=[tb])
            tldg = stage_load(f_gT.rearrange("(dk p) m -> p dk m", p=128),
                              atile[:], tgate=tprev)
            gw_t = lambda dk: atile[:, dk * E:(dk + 1) * E]
            b.wait("tensor", tldg, txn)
            for tq in range(4):
                for dk in range(DC):
                    b.do("tensor", lambda e, tq=tq, dk=dk: e.matmul(
                        out=PS[0:E, tq * 512:(tq + 1) * 512], lhsT=gw_t(dk),
                        rhs=b1v(dk, tq * 512, 512), start=(dk == 0), stop=(dk == DC - 1)))
            tgm = b.last("tensor", sT)
            b.wait("scalar", tgm)
            tsig = b.inc("scalar", lambda e: e.activation(out=SC[0:E, 0:TC],
                                                          in_=PS[0:E, 0:TC],
                                                          func=AF.Sigmoid), sA, 1)
            st_state["cons"][0] = tsig
            st_state["cons"][1] = tsig
            b.wait("tensor", tsig)
            tev = None
            for c in range(NCH):
                po = 2048 + (c % 2) * 512
                if tev is not None:
                    b.wait("tensor", tev)
                ttr = b.inc("tensor", lambda e, c=c, po=po: e.transpose(
                    out=PS[0:128, po:po + 16], in_=SC[0:E, c * 128:(c + 1) * 128],
                    identity=ident[0:E, 0:E]), sT, 1)
                b.wait("vector", ttr)
                tev = b.inc("vector", lambda e, c=c, po=po: e.tensor_copy(
                    out=gcomb[:, c * 16:(c + 1) * 16], in_=PS[0:128, po:po + 16]), sV, 1)
            tgtr = (sT, sT.n)
            st_state["cons"][0] = tgtr
            st_state["cons"][1] = tgtr
            GA = lambda c: gcomb[:, c * 16:(c + 1) * 16]
            CBc = lambda c: gcomb[:, 256 + c * 16:256 + (c + 1) * 16]
            MSc = lambda c: gcomb[:, 512 + c * 16:512 + (c + 1) * 16]
            mk = lambda c: m128[:, c:c + 1]
            s4 = lambda c: m128[:, 16 + c:17 + c]
            b.wait("vector", tev)
            t_ = b.inc("vector", lambda e: e.tensor_copy(out=gcomb[:, 256:512],
                                                         in_=gcomb[:, 0:256]), sV, 1)
            b.wait("vector", t_)
            t_ = b.inc("vector", lambda e: e.memset(m128[:, 16:32], 0.0), sV, 1)
            b.wait("vector", t_)
            for kk in range(4):
                for c in range(NCH):
                    b.do("vector", lambda e, c=c: e.tensor_reduce(
                        out=mk(c), in_=CBc(c), axis=AX.X, op=ALU.max))
                t_ = b.last("vector", sV); b.wait("vector", t_)
                for c in range(NCH):
                    b.do("vector", lambda e, c=c: e.tensor_tensor(
                        out=s4(c), in0=s4(c), in1=mk(c), op=ALU.add))
                t_ = b.last("vector", sV); b.wait("vector", t_)
                if kk < 3:
                    for c in range(NCH):
                        b.do("vector", lambda e, c=c: e.tensor_scalar(
                            out=MSc(c), in0=CBc(c), scalar1=mk(c), scalar2=None,
                            op0=ALU.is_ge))
                    t_ = b.last("vector", sV); b.wait("vector", t_)
                    for c in range(NCH):
                        b.do("vector", lambda e, c=c: e.scalar_tensor_tensor(
                            out=CBc(c), in0=MSc(c), scalar=BIGNEG, in1=CBc(c),
                            op0=ALU.mult, op1=ALU.add))
                    t_ = b.last("vector", sV); b.wait("vector", t_)
            for c in range(NCH):
                b.do("vector", lambda e, c=c: e.tensor_scalar(
                    out=MSc(c), in0=GA(c), scalar1=mk(c), scalar2=None, op0=ALU.is_ge))
            t_ = b.last("vector", sV); b.wait("vector", t_)
            for c in range(NCH):
                b.do("vector", lambda e, c=c: e.tensor_tensor(
                    out=CBc(c), in0=GA(c), in1=MSc(c), op=ALU.mult))
            t_ = b.last("vector", sV); b.wait("vector", t_)
            t_ = b.inc("vector", lambda e: e.reciprocal(out=m128[:, 16:32],
                                                        in_=m128[:, 16:32]), sV, 1)
            b.wait("vector", t_)
            for c in range(NCH):
                b.do("vector", lambda e, c=c: e.tensor_scalar(
                    out=CBc(c), in0=CBc(c), scalar1=s4(c), scalar2=None, op0=ALU.mult))
            tcomb = b.last("vector", sV)
            b.wait("tensor", tcomb)
            tevf = None
            for c in range(NCH):
                po = 2048 + (c % 2) * 512
                if tevf is not None:
                    b.wait("tensor", tevf)
                ttr = b.inc("tensor", lambda e, c=c, po=po: e.transpose(
                    out=PS[0:E, po:po + 128], in_=CBc(c), identity=ident[:]), sT, 1)
                b.wait("vector", ttr)
                tevf = b.inc("vector", lambda e, c=c, po=po: e.tensor_copy(
                    out=combfm_b[:, c * 128:(c + 1) * 128], in_=PS[0:E, po:po + 128]),
                    sV, 1)
            tcfm = tevf

            tq_done = [None] * 8
            stg_done = [None, None]
            hml = [None]
            tmm_hin = None
            tunit_done = tcfm
            t_b2 = {}
            t_b2[WSL[0][0]] = load_unit_w(0, 0, WSL[0][0], tgate=tprev)
            t_b2[WSL[0][1]] = load_unit_w(0, 1, WSL[0][1], tgate=tprev)
            woc_tok = [None]

            def cast_wout(u):
                wodram = unit_dram(u)[2]
                wo3 = wodram.rearrange("(r p) m -> p r m", p=128)
                for pc in range(4):
                    i = st_state["i"] % 2
                    st_state["i"] += 1
                    slot = SC[:, i * 1024:i * 1024 + 1024]
                    if st_state["cons"][i] is not None:
                        b.wait("sync", st_state["cons"][i])
                    tld = b.dma("sync", slot, wo3[:, pc:pc + 1, :], sStA if i == 0 else sStB)
                    b.wait("gpsimd", tld)
                    for hh in range(4):
                        b.wait("gpsimd", woc_tok[0])
                        tc = b.inc("gpsimd", lambda e, slot=slot, hh=hh: e.tensor_copy(
                            out=WOC, in_=slot[:, hh * 256:(hh + 1) * 256]), sGC, 1)
                        b.wait("gpsimd", tc)
                        tw = b.dma("gpsimd",
                                   wout_sc[u * H + pc * 128: u * H + (pc + 1) * 128,
                                           hh * 256:(hh + 1) * 256], WOC, sWoD)
                        woc_tok[0] = tw
                    st_state["cons"][i] = tc

            for u in range(NU):
                Au, Bu, Pu = WSL[u % 3]
                wdram, vdram, wodram, gated = unit_dram(u)
                if u + 1 < NU:
                    t_b2[Pu] = load_unit_w(u + 1, 0, Pu, tgate=tunit_done)
                b.wait("tensor", t_b2[Au], t_b2[Bu])
                thin_done = None
                tsv_map = {}
                for th in range(2):
                    if gated:
                        b.wait("tensor", tcfm)
                        if tq_done[6] is not None:
                            b.wait("tensor", tq_done[6])
                        if tq_done[7] is not None:
                            b.wait("tensor", tq_done[7])
                        for sub in range(2):
                            b.do("tensor", lambda e, u=u, th=th, sub=sub: e.matmul(
                                out=PS[:, 3072 + sub * 512:3072 + (sub + 1) * 512],
                                lhsT=esel[:, u * 128:(u + 1) * 128],
                                rhs=combfm_b[:, th * 1024 + sub * 512:
                                             th * 1024 + (sub + 1) * 512],
                                start=True, stop=True))
                        tcbm = b.last("tensor", sT)
                        b.wait("vector", tcbm)
                        tcb = b.inc("vector", lambda e: e.tensor_copy(
                            out=cbe[:, 0:1024], in_=PS[:, 3072:4096]), sV, 1)
                        tq_done[6] = tcb
                        tq_done[7] = tcb
                    b.wait("scalar", hml[0])
                    for hc in range(4):
                        for t2 in range(2):
                            bk = (hc * 2 + t2) % 6
                            if tq_done[bk] is not None:
                                b.wait("tensor", tq_done[bk])
                            for dk in range(DC):
                                b.do("tensor", lambda e, u=u, th=th, hc=hc, t2=t2,
                                     dk=dk, Au=Au, bk=bk: e.matmul(
                                    out=bank(bk),
                                    lhsT=b2s(Au, dk * 512 + hc * 128, 128),
                                    rhs=b1v(dk, th * 1024 + t2 * 512, 512),
                                    start=(dk == 0), stop=(dk == DC - 1)))
                            tmm = b.last("tensor", sT)
                            b.wait("scalar", tmm)
                            tsv = b.inc("scalar", lambda e, hc=hc, t2=t2, bk=bk:
                                        e.activation(
                                out=SV[:, hc * 1024 + t2 * 512: hc * 1024 + t2 * 512 + 512], in_=bank(bk),
                                func=AF.Silu), sA, 1)
                            tq_done[bk] = tsv
                            tsv_map[(hc, t2)] = tsv
                    thin_done = tsv
                    for hc in range(4):
                        for t2 in range(2):
                            bk = (hc * 2 + t2) % 6
                            if tq_done[bk] is not None:
                                b.wait("tensor", tq_done[bk])
                            for dk in range(DC):
                                b.do("tensor", lambda e, u=u, th=th, hc=hc, t2=t2,
                                     dk=dk, Bu=Bu, bk=bk: e.matmul(
                                    out=bank(bk),
                                    lhsT=b2s(Bu, dk * 512 + hc * 128, 128),
                                    rhs=b1v(dk, th * 1024 + t2 * 512, 512),
                                    start=(dk == 0), stop=(dk == DC - 1)))
                            tmm = b.last("tensor", sT)
                            b.wait("vector", tmm, hml[0])
                            if gated:
                                b.wait("vector", tcb)
                                tvc = b.inc("vector", lambda e, t2=t2, bk=bk:
                                            e.tensor_tensor(
                                    out=VCX, in0=bank(bk),
                                    in1=cbe[:, t2 * 512:(t2 + 1) * 512],
                                    op=ALU.mult), sV, 1)
                            else:
                                tvc = b.inc("vector", lambda e, bk=bk: e.tensor_copy(
                                    out=VCX, in_=bank(bk)), sV, 1)
                            tq_done[bk] = tvc
                            ssl = (hc * 2 + t2) % 2
                            b.wait("vector", tvc, tsv_map[(hc, t2)])
                            if stg_done[ssl] is not None:
                                b.wait("vector", stg_done[ssl])
                            hml[0] = th_ = b.inc("vector", lambda e, hc=hc, t2=t2, ssl=ssl:
                                        e.tensor_tensor(
                                out=stage[:, ssl * 512:(ssl + 1) * 512],
                                in0=SV[:, hc * 1024 + t2 * 512: hc * 1024 + t2 * 512 + 512], in1=VCX,
                                op=ALU.mult), sV, 1)
                            b.wait("gpsimd", th_)
                            tdma = b.dma("gpsimd",
                                         h_sc[u * H + hc * 128: u * H + (hc + 1) * 128,
                                              th * 1024 + t2 * 512:
                                              th * 1024 + (t2 + 1) * 512],
                                         stage[:, ssl * 512:(ssl + 1) * 512], sHo[ssl])
                            stg_done[ssl] = tdma
                    if th == 1 and u + 1 < NU:
                        t_b2[Au] = load_unit_w(u + 1, 1, Au, tgate=thin_done)
                if first_call:
                    cast_wout(u)
                tunit_done = (sT, sT.n)
            th_all = [(s_, s_.n) for s_ in sHo]
            two_all = (sWoD, sWoD.n)

            wsl_tok = {0: None, 1: None}
            wsl_used = {0: None, 1: None}

            def load_wout_b2(u, sl):
                if wsl_used[sl] is not None:
                    b.wait("sync", wsl_used[sl])
                if first_call:
                    b.wait("sync", two_all)
                dv = wout_sc[u * H:(u + 1) * H, :].rearrange("(r p) m -> p r m", p=128)
                wsl_tok[sl] = b.dma("sync", B2[:, sl * 4096:(sl + 1) * 4096]
                                    .rearrange("p (r m) -> p r m", m=1024), dv,
                                    sWA if sl == 0 else sWB)

            tev_tt = None
            wsl_used[0] = tunit_done
            wsl_used[1] = tunit_done
            hs_tok = [hml[0]] * 8
            hsv = lambda j: SV[:, j * 512:(j + 1) * 512]
            for tt in range(4):
                load_wout_b2(0, 0)
                first_mm = True
                items = [(u, hc) for u in range(NU) for hc in range(4)]

                def hload(i, tt=tt):
                    u, hc = items[i]
                    hsl = i % 8
                    if hs_tok[hsl] is not None:
                        b.wait("sync", hs_tok[hsl])
                    b.wait("sync", *th_all)
                    return b.dma("sync", hsv(hsl),
                                 h_sc[u * H + hc * 128: u * H + (hc + 1) * 128,
                                      tt * 512:(tt + 1) * 512], sHsl[hsl])
                tldhs = [hload(j) for j in range(6)]
                for i, (u, hc) in enumerate(items):
                    sl = u % 2
                    if hc == 0 and u + 1 < NU:
                        load_wout_b2(u + 1, 1 - sl)
                    if i + 6 < len(items):
                        tldhs.append(hload(i + 6))
                    hsl = i % 8
                    b.wait("tensor", wsl_tok[sl], tldhs[i])
                    for dc in range(DC):
                        b.do("tensor", lambda e, u=u, hc=hc, dc=dc, sl=sl,
                             hsl=hsl, fm=first_mm: e.matmul(
                            out=bank(dc),
                            lhsT=B2[:, sl * 4096 + hc * 1024 + dc * 128:
                                    sl * 4096 + hc * 1024 + (dc + 1) * 128],
                            rhs=hsv(hsl),
                            start=fm, stop=(u == NU - 1 and hc == 3)))
                    first_mm = False
                    thc = b.last("tensor", sT)
                    hs_tok[hsl] = thc
                    if hc == 3:
                        wsl_used[sl] = thc
                tmm_done = (sT, sT.n)
                b.wait("vector", tmm_done)
                for dc in range(DC):
                    b.do("vector", lambda e, tt=tt, dc=dc: e.tensor_tensor(
                        out=xv(dc, tt * 512, 512), in0=bank(dc),
                        in1=xv(dc, tt * 512, 512), op=ALU.add))
                tev_tt = b.last("vector", sV)
                b.wait("tensor", tev_tt)
                wsl_used[0] = tev_tt
                wsl_used[1] = tev_tt
            return tev_tt


        sVt = [Sem(es, nc, f"sVt{i}") for i in range(2)]
        sKpo = Sem(es, nc, "sKpo")
        sKp = Sem(es, nc, "sKp")
        sVs = [Sem(es, nc, f"sVs{i}") for i in range(2)]
        sZo = [Sem(es, nc, f"sZo{i}") for i in range(2)]
        sZi = Sem(es, nc, "sZi")
        sH1 = [Sem(es, nc, f"sH1{i}") for i in range(2)]
        sF2 = [Sem(es, nc, f"sF2{i}") for i in range(4)]
        sAR2 = Sem(es, nc, "sAR2")

        def emit_s3(tprev):
            def post_rows(tq, tm2):
                s0 = tq * 512
                b.wait("vector", tm2)
                t_ = b.inc("vector", lambda e: e.tensor_tensor(out=ROW2, in0=VAR,
                                                               in1=VAR, op=ALU.mult), sV, 1)
                b.wait("vector", t_)
                t_ = b.inc("vector", lambda e: e.tensor_scalar(
                    out=ROW2, in0=ROW2, scalar1=-LN_EPS, scalar2=1.0,
                    op0=ALU.mult, op1=ALU.add), sV, 1)
                b.wait("scalar", t_)
                t_ = b.inc("scalar", lambda e: e.activation(out=ROW2, in_=ROW2,
                                                            func=AF.Ln, bias=LN_EPS), sA, 1)
                b.wait("scalar", t_)
                t_ = b.inc("scalar", lambda e: e.activation(out=ROW2, in_=ROW2,
                                                            func=AF.Exp, scale=-0.5), sA, 1)
                b.wait("vector", t_)
                t_ = b.inc("vector", lambda e: e.tensor_tensor(out=MU, in0=VAR,
                                                               in1=ROW2, op=ALU.mult), sV, 1)
                b.wait("vector", t_)
                tm = b.inc("vector", lambda e: e.tensor_tensor(out=ROW2, in0=TROW,
                                                               in1=ROW2, op=ALU.mult), sV, 1)
                b.wait("tensor", tm)
                b.do("tensor", lambda e, s0=s0: e.matmul(
                    out=PS[:, s0:s0 + 512], lhsT=ones_f[:],
                    rhs=MU, start=True, stop=True))
                b.do("tensor", lambda e, s0=s0: e.matmul(
                    out=PS[:, 2048 + s0:2048 + s0 + 512], lhsT=ones_f[:],
                    rhs=ROW2, start=True, stop=True))
                tb2_ = b.last("tensor", sT)
                b.wait("vector", tb2_)
                b.do("vector", lambda e, s0=s0: e.tensor_copy(
                    out=SB_[:, s0:s0 + 512], in_=PS[:, s0:s0 + 512]))
                tz = b.inc("vector", lambda e, s0=s0: e.tensor_copy(
                    out=M2B[:, s0:s0 + 512], in_=PS[:, 2048 + s0:2048 + s0 + 512]),
                    sV, 1)
                b.wait("tensor", tz)
                return VAR, TROW, tz

            tbp = emit_ln_stats_rolled(lambda dc: xv(dc, 0, TC), after=[tprev],
                                       sb_dst=SBP, m2_dst=M2BP, post_rows=post_rows)
            ta1 = emit_norm(lambda dc: xv(dc, 0, TC), lambda dc: b1v(dc, 0, TC),
                            lnw=True, lnwoff=VEC["b_ln1w"], after=[tbp])

            tqw = stage_load(b_qT.rearrange("(dk p) m -> p dk m", p=128),
                             stage[:, 0:512], tgate=tprev)
            tkw = stage_load(b_kT.rearrange("(dk p) m -> p dk m", p=128),
                             stage[:, 512:1024], tgate=tprev)
            qw_t = lambda dk: stage[:, dk * S:(dk + 1) * S]
            kw_t = lambda dk: stage[:, 512 + dk * S:512 + (dk + 1) * S]
            b.wait("tensor", tqw, tkw, ta1)
            for tq in range(4):
                for dk in range(DC):
                    b.do("tensor", lambda e, tq=tq, dk=dk: e.matmul(
                        out=PS[0:S, tq * 512:(tq + 1) * 512], lhsT=qw_t(dk),
                        rhs=b1v(dk, tq * 512, 512), start=(dk == 0), stop=(dk == DC - 1)))
            for tq in range(4):
                for dk in range(DC):
                    b.do("tensor", lambda e, tq=tq, dk=dk: e.matmul(
                        out=PS[0:S, 2048 + tq * 512:2048 + (tq + 1) * 512], lhsT=kw_t(dk),
                        rhs=b1v(dk, tq * 512, 512), start=(dk == 0), stop=(dk == DC - 1)))
            tqk = b.last("tensor", sT)

            def elu1(psoff, bias_col, dst_bf):
                b.wait("vector", tqk)
                t1 = b.inc("vector", lambda e, psoff=psoff, bias_col=bias_col:
                           e.tensor_scalar(
                    out=ksqk[:], in0=PS[0:S, psoff:psoff + TC], scalar1=bias_col,
                    scalar2=0.0, op0=ALU.add, op1=ALU.max), sV, 1)
                t2 = b.inc("vector", lambda e, psoff=psoff, bias_col=bias_col:
                           e.tensor_scalar(
                    out=SC[0:S, 0:TC], in0=PS[0:S, psoff:psoff + TC], scalar1=bias_col,
                    scalar2=0.0, op0=ALU.add, op1=ALU.min), sV, 1)
                b.wait("scalar", t2)
                t3 = b.inc("scalar", lambda e: e.activation(out=SC[0:S, 0:TC],
                                                            in_=SC[0:S, 0:TC],
                                                            func=AF.Exp), sA, 1)
                b.wait("vector", t3, t1)
                t4 = b.inc("vector", lambda e, dst_bf=dst_bf: e.tensor_tensor(
                    out=dst_bf, in0=SC[0:S, 0:TC], in1=ksqk[:], op=ALU.add), sV, 1)
                return t4

            tQ = elu1(0, v64c("b_qb"), qb16[:])
            b.wait("vector", tQ)
            tK = elu1(2048, v64c("b_kb"), kb16[:])
            b.wait("vector", tK)
            t_ = b.inc("vector", lambda e: e.memset(wful_t[0:S, 0:SEG], 1.0), sV, 1)
            b.wait("vector", t_)
            tks = emit_scan(lambda c0, n: ksqk[:, c0:c0 + n], wful_t[0:S, 0:SEG],
                            lambda c0, n: kb16[:, c0:c0 + n], TC)
            b.wait("vector", tks)
            tkst = b.inc("vector", lambda e: e.tensor_copy(out=kvrun[:, 1024:1025],
                                                           in_=ksqk[:, TC - 1:TC]), sV, 1)
            b.wait("vector", tkst)
            tqks = b.inc("vector", lambda e: e.tensor_tensor(out=ksqk[:], in0=qb16[:],
                                                             in1=ksqk[:], op=ALU.mult),
                         sV, 1)
            b.wait("tensor", tqks)
            for c in range(NCH):
                b.do("tensor", lambda e, c=c: e.matmul(
                    out=PS[:, 1024 + c:1025 + c], lhsT=ksqk[:, c * CH:(c + 1) * CH],
                    rhs=onesc_b[0:S, :], start=True, stop=True))
            tden = b.last("tensor", sT)
            b.wait("vector", tden)
            tdenl = b.inc("vector", lambda e: e.tensor_scalar(
                out=denloc[:], in0=PS[:, 1024:1040], scalar1=ATTN_EPS, scalar2=None,
                op0=ALU.add), sV, 1)

            st_state["cons"][0] = tK
            st_state["cons"][1] = tK
            v3 = b_vT.rearrange("(dk p) m -> p dk m", p=128)
            tvw = None
            for pc in range(8):
                tvw = stage_load(v3[:, pc:pc + 1, :], B2[:, pc * 1024:(pc + 1) * 1024],
                                 tgate=tprev)
            tvb = stage_load(b_vbrow, vbrow_b[0:1, :], tgate=tprev)
            vw_t = lambda dk, dh: B2[:, dk * 1024 + dh * 512: dk * 1024 + (dh + 1) * 512]

            f2c_state = {"woc": (sWoD, sWoD.n)}

            def emit_f2cast_piece(pc):
                i = st_state["i"] % 2
                st_state["i"] += 1
                slot = SC[:, i * 1024:i * 1024 + 1024]
                if st_state["cons"][i] is not None:
                    b.wait("sync", st_state["cons"][i])
                tld = b.dma("sync", slot,
                            b_f2T[pc * 128:(pc + 1) * 128, :], sStA if i == 0 else sStB)
                b.wait("gpsimd", tld)
                tcwoc = None
                for hh in range(4):
                    b.wait("gpsimd", f2c_state["woc"])
                    tcwoc = b.inc("gpsimd", lambda e, slot=slot, hh=hh: e.tensor_copy(
                        out=WOC, in_=slot[:, hh * 256:(hh + 1) * 256]), sGC, 1)
                    b.wait("gpsimd", tcwoc)
                    f2c_state["woc"] = b.dma(
                        "gpsimd", f2_sc[pc * 128:(pc + 1) * 128,
                                        hh * 256:(hh + 1) * 256], WOC, sWoD)
                st_state["cons"][i] = tcwoc

            b.wait("vector", tkst)
            b.do("vector", lambda e: e.memset(kvrun[:, 1025:1040], 0.0))
            t_kvz = b.inc("vector", lambda e: e.memset(kvrun[:, 0:1024], 0.0), sV, 1)
            tv_ev = [None, None]
            tkv_mm = None
            tktm_prev = None
            tkvadd = None
            b.wait("tensor", tvw, tvb)
            for c in range(NCH):
                for dh in range(2):
                    for dk in range(DC):
                        b.do("tensor", lambda e, c=c, dh=dh, dk=dk: e.matmul(
                            out=PS[:, 2048 + dh * 512:2048 + (dh + 1) * 512],
                            lhsT=b1v(dk, c * CH, CH), rhs=vw_t(dk, dh),
                            start=(dk == 0), stop=False))
                    b.do("tensor", lambda e, c=c, dh=dh: e.matmul(
                        out=PS[:, 2048 + dh * 512:2048 + (dh + 1) * 512],
                        lhsT=ones_b[:, 0:CH], rhs=vbrow_b[0:1, dh * 512:(dh + 1) * 512],
                        start=False, stop=True))
                tvmm = b.last("tensor", sT)
                vsl = c % 2
                b.wait("vector", tvmm, tv_ev[vsl])
                tvev = b.inc("vector", lambda e, vsl=vsl: e.tensor_copy(
                    out=SV[:, vsl * 1024:(vsl + 1) * 1024], in_=PS[:, 2048:3072]), sV, 1)
                b.wait("gpsimd", tvev)
                tvout = b.dma("gpsimd", vtm_sc[c * CH:(c + 1) * CH, :],
                              SV[:, vsl * 1024:(vsl + 1) * 1024], sVt[vsl])
                tv_ev[vsl] = tvout
                b.wait("vector", tktm_prev)
                tkf = b.inc("vector", lambda e, c=c: e.tensor_copy(
                    out=agkv[0:S, 0:CH], in_=kb16[:, c * CH:(c + 1) * CH]), sV, 1)
                b.wait("tensor", tkf, tkv_mm)
                tktr = b.inc("tensor", lambda e: e.transpose(
                    out=PS[0:CH, 1536:1600], in_=agkv[0:S, 0:CH],
                    identity=ident[0:S, 0:S]), sT, 1)
                b.wait("vector", tktr)
                tktm = b.inc("vector", lambda e: e.tensor_copy(out=ktm_b[:],
                                                               in_=PS[0:CH, 1536:1600]),
                             sV, 1)
                tktm_prev = tktm
                b.wait("vector", t_kvz if c == 0 else tkvadd)
                tkpc = b.inc("vector", lambda e: e.tensor_copy(out=kvpref_b[:],
                                                               in_=kvrun[:, 0:1024]),
                             sV, 1)
                b.wait("gpsimd", tkpc)
                tkpo = b.dma("gpsimd", kvpref_sc[c * S:(c + 1) * S, :], kvpref_b[:],
                             sKpo)
                b.wait("tensor", tktm, tvev)
                for dh in range(2):
                    b.do("tensor", lambda e, dh=dh, vsl=vsl: e.matmul(
                        out=PS[0:S, 3072 + dh * 512:3072 + (dh + 1) * 512],
                        lhsT=ktm_b[:], rhs=SV[:, vsl * 1024 + dh * 512:
                                              vsl * 1024 + (dh + 1) * 512],
                        start=True, stop=True))
                tkv_mm = b.last("tensor", sT)
                b.wait("vector", tkv_mm, tkpo)
                tkvadd = b.inc("vector", lambda e: e.tensor_tensor(
                    out=kvrun[:, 0:1024], in0=kvrun[:, 0:1024],
                    in1=PS[0:S, 3072:4096], op=ALU.add), sV, 1)
                emit_f2cast_piece(2 * c)
                emit_f2cast_piece(2 * c + 1)
            tpassA = tkvadd
            tf2w_all = (sWoD, sWoD.n)

            b.wait("vector", tpassA, tkst)
            tpd = None
            for r in range(NCORE):
                for pp in range(2):
                    t_ = b.inc("vector", lambda e, r=r, pp=pp: e.tensor_scalar(
                        out=agkv[:, 0:520], in0=kvrun[:, pp * 520:(pp + 1) * 520],
                        scalar1=v64[:, V64["ohc"] + r:V64["ohc"] + r + 1], scalar2=None,
                        op0=ALU.mult), sV, 1)
                    b.wait("gpsimd", t_)
                    tpd = b.dma("gpsimd", ag2_in[:, r * 1040 + pp * 520:
                                                 r * 1040 + (pp + 1) * 520],
                                agkv[:, 0:520], sAR2)
                    b.wait("vector", tpd)
            b.wait("gpsimd", tpd)
            tcc2 = b.inc("gpsimd", lambda e: e.collective_compute(
                "AllReduce", ALU.add, replica_groups=[list(range(NCORE))],
                ins=[ag2_in], outs=[ag2_out]), sC2, 1)
            b.wait("gpsimd", tcc2)
            b.wait("vector", tcc2)
            tacc = b.inc("vector", lambda e: e.memset(kvrun[:], 0.0), sV, 1)
            for r in range(NCORE):
                for pp in range(2):
                    b.wait("gpsimd", tacc)
                    tld = b.dma("gpsimd", agkv[:, 0:520],
                                ag2_out[:, r * 1040 + pp * 520:r * 1040 + (pp + 1) * 520],
                                sAR2)
                    b.wait("vector", tld)
                    tacc = b.inc("vector", lambda e, r=r, pp=pp: e.scalar_tensor_tensor(
                        out=kvrun[:, pp * 520:(pp + 1) * 520], in0=agkv[:, 0:520],
                        scalar=v64[:, V64["kv_mask"] + r:V64["kv_mask"] + r + 1],
                        in1=kvrun[:, pp * 520:(pp + 1) * 520],
                        op0=ALU.mult, op1=ALU.add), sV, 1)
            b.wait("vector", tacc)
            tkvin = b.inc("vector", lambda e: e.tensor_copy(out=kvin_b[:, 0:1040],
                                                            in_=kvrun[:, 0:1040]), sV, 1)


            tnum_ev = None
            tkpin = None
            tkpin_d = None
            tkpo_all = (sKpo, sKpo.n)
            tvt_all = [(s_, s_.n) for s_ in sVt]
            tvs = [None, None]
            b.wait("sync", tkpo_all, *tvt_all)
            for c in range(NCH):
                b.wait("sync", tkpin)
                tkpin_d = b.dma("sync", kvpref_b[:], kvpref_sc[c * S:(c + 1) * S, :], sKp)
                vsl = c % 2
                b.wait("sync", tvs[vsl])
                tvsd = b.dma("sync", SV[:, 2048 + vsl * 1024:2048 + (vsl + 1) * 1024],
                             vtm_sc[c * CH:(c + 1) * CH, :], sVs[vsl])
                b.wait("tensor", tkvin, tdenl)
                tdmm = b.inc("tensor", lambda e, c=c: e.matmul(
                    out=PS[:, 1024:1025], lhsT=qb16[:, c * CH:(c + 1) * CH],
                    rhs=kvin_b[:, 1024:1025], start=True, stop=True), sT, 1)
                b.wait("vector", tdmm, tnum_ev)
                t_ = b.inc("vector", lambda e, c=c: e.tensor_tensor(
                    out=den1[:, 0:1], in0=PS[:, 1024:1025], in1=denloc[:, c:c + 1],
                    op=ALU.add), sV, 1)
                b.wait("vector", t_)
                t_ = b.inc("vector", lambda e: e.reciprocal(out=den1[:, 1:2],
                                                            in_=den1[:, 0:1]), sV, 1)
                b.wait("tensor", t_)
                ttr1 = b.inc("tensor", lambda e: e.transpose(
                    out=PS[0:1, 1152:1280], in_=den1[:, 1:2], identity=ident[:]), sT, 1)
                b.wait("vector", ttr1)
                t_ = b.inc("vector", lambda e: e.tensor_copy(out=rrow[0:1, 0:128],
                                                             in_=PS[0:1, 1152:1280]),
                           sV, 1)
                b.wait("tensor", t_)
                tbc1 = b.inc("tensor", lambda e: e.matmul(
                    out=PS[0:S, 1280:1408], lhsT=ones_f[0:1, 0:S], rhs=rrow[0:1, 0:128],
                    start=True, stop=True), sT, 1)
                b.wait("vector", tbc1)
                t_ = b.inc("vector", lambda e: e.tensor_copy(out=agkv[0:S, 128:256],
                                                             in_=PS[0:S, 1280:1408]),
                           sV, 1)
                b.wait("vector", t_)
                tqp = b.inc("vector", lambda e, c=c: e.tensor_tensor(
                    out=qpb[:], in0=qb16[:, c * CH:(c + 1) * CH],
                    in1=agkv[0:S, 128:256], op=ALU.mult), sV, 1)
                b.wait("tensor", tqp)
                tamm = b.inc("tensor", lambda e, c=c: e.matmul(
                    out=PS[:, 1408:1536], lhsT=kb16[:, c * CH:(c + 1) * CH], rhs=qpb[:],
                    start=True, stop=True), sT, 1)
                b.wait("vector", tamm)
                tat = b.inc("vector", lambda e: e.tensor_tensor(
                    out=atile[:], in0=PS[:, 1408:1536], in1=tri[:], op=ALU.mult), sV, 1)
                b.wait("tensor", tkpin_d, tat, tvsd)
                for dh2 in range(DC):
                    b.do("tensor", lambda e, dh2=dh2: e.matmul(
                        out=PS[:, dh2 * 128:(dh2 + 1) * 128],
                        lhsT=kvpref_b[:, dh2 * 128:(dh2 + 1) * 128], rhs=qpb[:],
                        start=True, stop=False))
                    b.do("tensor", lambda e, dh2=dh2: e.matmul(
                        out=PS[:, dh2 * 128:(dh2 + 1) * 128],
                        lhsT=kvin_b[:, dh2 * 128:(dh2 + 1) * 128], rhs=qpb[:],
                        start=False, stop=False))
                    b.do("tensor", lambda e, dh2=dh2, vsl=vsl: e.matmul(
                        out=PS[:, dh2 * 128:(dh2 + 1) * 128],
                        lhsT=SV[:, 2048 + vsl * 1024 + dh2 * 128:
                                2048 + vsl * 1024 + (dh2 + 1) * 128],
                        rhs=atile[:], start=False, stop=True))
                tnmm = b.last("tensor", sT)
                tkpin = tnmm
                tvs[vsl] = tnmm
                b.wait("vector", tnmm)
                tnum_ev = b.inc("vector", lambda e, c=c: e.tensor_copy(
                    out=B1[:].rearrange("p (dc t) -> p dc t", t=TC)
                    [:, :, c * CH:(c + 1) * CH],
                    in_=PS[:, 0:1024].rearrange("p (dc i) -> p dc i", i=CH)), sV, 1)
                b.wait("tensor", tnum_ev)
            tpassB = tnum_ev

            st_state["cons"][0] = tpassB
            st_state["cons"][1] = tpassB
            o3 = b_oT.rearrange("(dk p) m -> p dk m", p=128)
            tow = None
            for pc in range(8):
                tow = stage_load(o3[:, pc:pc + 1, :], B2[:, pc * 1024:(pc + 1) * 1024],
                                 tgate=tpassA)
            ow_t = lambda dk, dc: B2[:, dk * 1024 + dc * 128: dk * 1024 + (dc + 1) * 128]
            tzo = [None, None]
            tox = None
            b.wait("tensor", tow, tpassB)
            for tq in range(4):
                for dc in range(DC):
                    for dk in range(DC):
                        b.do("tensor", lambda e, tq=tq, dc=dc, dk=dk: e.matmul(
                            out=bank(dc), lhsT=ow_t(dk, dc),
                            rhs=b1v(dk, tq * 512, 512),
                            start=(dk == 0), stop=(dk == DC - 1)))
                tomm = b.last("tensor", sT)
                b.wait("vector", tomm, tox)
                for dc in range(DC):
                    zsl = dc % 2
                    t_ = b.inc("vector", lambda e, tq=tq, dc=dc: e.tensor_tensor(
                        out=SC[:, 0:512], in0=xv(dc, tq * 512, 512),
                        in1=SBP[:, tq * 512:(tq + 1) * 512], op=ALU.mult), sV, 1)
                    b.wait("vector", t_)
                    t_ = b.inc("vector", lambda e, tq=tq: e.tensor_tensor(
                        out=SC[:, 0:512], in0=SC[:, 0:512],
                        in1=M2BP[:, tq * 512:(tq + 1) * 512], op=ALU.subtract), sV, 1)
                    b.wait("vector", t_, tzo[zsl])
                    tz_ = b.inc("vector", lambda e, dc=dc, zsl=zsl: e.scalar_tensor_tensor(
                        out=stage[:, zsl * 512:(zsl + 1) * 512], in0=bank(dc),
                        scalar=vec8[:, VEC["b_ob"] + dc:VEC["b_ob"] + dc + 1],
                        in1=SC[:, 0:512], op0=ALU.add, op1=ALU.add), sV, 1)
                    b.wait("gpsimd", tz_)
                    tzd = b.dma("gpsimd", z_sc.rearrange("(dc p) t -> p dc t", p=128)
                                [:, dc:dc + 1, tq * 512:(tq + 1) * 512],
                                stage[:, zsl * 512:(zsl + 1) * 512], sZo[zsl])
                    tzo[zsl] = tzd
                    t_ = b.inc("vector", lambda e, tq=tq, dc=dc: e.tensor_tensor(
                        out=xv(dc, tq * 512, 512), in0=xv(dc, tq * 512, 512),
                        in1=SC[:, 0:512], op=ALU.add), sV, 1)
                    b.wait("vector", t_)
                    tox = b.inc("vector", lambda e, tq=tq, dc=dc: e.scalar_tensor_tensor(
                        out=xv(dc, tq * 512, 512), in0=bank(dc),
                        scalar=vec8[:, VEC["b_ob"] + dc:VEC["b_ob"] + dc + 1],
                        in1=xv(dc, tq * 512, 512), op0=ALU.add, op1=ALU.add), sV, 1)
                b.wait("tensor", tox)
            tzo_all = [(s_, s_.n) for s_ in sZo]

            b.wait("scalar", tf2w_all)
            tprevz = None
            for dc in range(DC):
                if tprevz is not None:
                    b.wait("sync", tprevz)
                else:
                    b.wait("sync", tox, *tzo_all)
                tldz = b.dma("sync", SCB[:, 0:2048],
                             z_sc[dc * 128:(dc + 1) * 128, :], sZi)
                b.wait("scalar", tldz)
                tsq = b.inc("scalar", lambda e: e.activation(out=SQR, in_=SCB[:, 0:2048],
                                                             func=AF.Square), sA, 1)
                b.wait("tensor", tldz)
                for tq in range(4):
                    b.do("tensor", lambda e, tq=tq, dc=dc: e.matmul(
                        out=PS[0:1, tq * 512:(tq + 1) * 512], lhsT=onesc_b[:],
                        rhs=SCB[:, tq * 512:(tq + 1) * 512],
                        start=(dc == 0), stop=(dc == DC - 1)))
                b.wait("tensor", tsq)
                for tq in range(4):
                    b.do("tensor", lambda e, tq=tq, dc=dc: e.matmul(
                        out=PS[0:1, 2048 + tq * 512:2048 + (tq + 1) * 512],
                        lhsT=onesc_b[:], rhs=SQR[:, tq * 512:(tq + 1) * 512],
                        start=(dc == 0), stop=(dc == DC - 1)))
                tprevz = b.last("tensor", sT)
                b.wait("scalar", tprevz)
            tbz = emit_rows_bcast(tprevz, 2048)
            tnz = None
            tzi2 = None
            b.wait("sync", tprevz)
            for dc in range(DC):
                for tp in range(4):
                    zsl = tp % 2
                    b.wait("sync", tnz, tzi2)
                    tldz = b.dma("sync", stage[:, zsl * 512:(zsl + 1) * 512],
                                 z_sc[dc * 128:(dc + 1) * 128, tp * 512:(tp + 1) * 512],
                                 sZi)
                    tzi2 = tldz
                    b.wait("vector", tldz, tbz)
                    t_ = b.inc("vector", lambda e, dc=dc, tp=tp, zsl=zsl: e.tensor_tensor(
                        out=b1v(dc, tp * 512, 512),
                        in0=stage[:, zsl * 512:(zsl + 1) * 512],
                        in1=SB_[:, tp * 512:(tp + 1) * 512], op=ALU.mult), sV, 1)
                    b.wait("vector", t_)
                    t_ = b.inc("vector", lambda e, dc=dc, tp=tp: e.tensor_tensor(
                        out=b1v(dc, tp * 512, 512), in0=b1v(dc, tp * 512, 512),
                        in1=M2B[:, tp * 512:(tp + 1) * 512], op=ALU.subtract), sV, 1)
                    b.wait("vector", t_)
                    tnz = b.inc("vector", lambda e, dc=dc, tp=tp: e.tensor_scalar(
                        out=b1v(dc, tp * 512, 512), in0=b1v(dc, tp * 512, 512),
                        scalar1=vec8[:, VEC["b_ln2w"] + dc:VEC["b_ln2w"] + dc + 1],
                        scalar2=vec8[:, VEC["b_ln2b"] + dc:VEC["b_ln2b"] + dc + 1],
                        op0=ALU.mult, op1=ALU.add), sV, 1)
            ta2 = tnz

            st_state["cons"][0] = ta2
            st_state["cons"][1] = ta2
            th1o = [None, None]
            tf1_used = None
            tbk = [None] * 8
            for hq in range(4):
                f13 = b_f1T[:, hq * 1024:(hq + 1) * 1024].rearrange(
                    "(dk p) m -> p dk m", p=128)
                tf1 = None
                for pc in range(8):
                    tf1 = stage_load(f13[:, pc:pc + 1, :],
                                     B2[:, pc * 1024:(pc + 1) * 1024],
                                     tgate=tf1_used if hq > 0 else ta2)
                f1_t = lambda dk, hr: B2[:, dk * 1024 + hr * 128: dk * 1024 + (hr + 1) * 128]
                b.wait("tensor", tf1, ta2)
                for hr in range(8):
                    for tq in range(4):
                        bk = (hr * 4 + tq) % 8
                        if tbk[bk] is not None:
                            b.wait("tensor", tbk[bk])
                        for dk in range(DC):
                            b.do("tensor", lambda e, hq=hq, hr=hr, tq=tq, dk=dk, bk=bk:
                                 e.matmul(
                                out=bank(bk), lhsT=f1_t(dk, hr),
                                rhs=b1v(dk, tq * 512, 512),
                                start=(dk == 0), stop=(dk == DC - 1)))
                        tmm = b.last("tensor", sT)
                        hsl = (hr * 4 + tq) % 2
                        b.wait("scalar", tmm, th1o[hsl])
                        tg_ = b.inc("scalar", lambda e, hq=hq, hr=hr, bk=bk, hsl=hsl:
                                    e.activation(
                            out=stage[:, hsl * 512:(hsl + 1) * 512], in_=bank(bk),
                            func=AF.Gelu,
                            bias=f1b_sb[:, hq * 8 + hr:hq * 8 + hr + 1]), sA, 1)
                        tbk[bk] = tg_
                        b.wait("gpsimd", tg_)
                        th1d = b.dma("gpsimd",
                                     h1_sc[hq * 1024 + hr * 128:
                                           hq * 1024 + (hr + 1) * 128,
                                           tq * 512:(tq + 1) * 512],
                                     stage[:, hsl * 512:(hsl + 1) * 512], sH1[hsl])
                        th1o[hsl] = th1d
                tf1_used = (sT, sT.n)
            th1_all = [(s_, s_.n) for s_ in sH1]

            tf2sl = [tf1_used] * 4
            thsl2 = [None] * 8
            tevx = None
            hsv = lambda j: SV[:, j * 512:(j + 1) * 512]
            for tt in range(4):
                first_mm = True

                def f2load(kq, tt=tt):
                    fsl = kq % 4
                    b.wait("sync", tf2sl[fsl], tf2w_all)
                    tfld = b.dma("sync", B2[:, fsl * 1024:(fsl + 1) * 1024],
                                 f2_sc[kq * 128:(kq + 1) * 128, :], sF2[fsl])
                    hsl = kq % 8
                    b.wait("sync", thsl2[hsl], *th1_all)
                    tldh = b.dma("sync", hsv(hsl),
                                 h1_sc[kq * 128:(kq + 1) * 128,
                                       tt * 512:(tt + 1) * 512], sHsl[hsl])
                    return (tfld, tldh)
                lds = [f2load(j) for j in range(3)]
                for kq in range(32):
                    if kq + 3 < 32:
                        lds.append(f2load(kq + 3))
                    fsl = kq % 4
                    hsl = kq % 8
                    b.wait("tensor", lds[kq][0], lds[kq][1], tevx)
                    for dc in range(DC):
                        b.do("tensor", lambda e, kq=kq, dc=dc, fsl=fsl, hsl=hsl,
                             fm=first_mm: e.matmul(
                            out=bank(dc),
                            lhsT=B2[:, fsl * 1024 + dc * 128:
                                    fsl * 1024 + (dc + 1) * 128],
                            rhs=hsv(hsl),
                            start=fm, stop=(kq == 31)))
                    first_mm = False
                    tmm = b.last("tensor", sT)
                    tf2sl[fsl] = tmm
                    thsl2[hsl] = tmm
                tmm_done = (sT, sT.n)
                b.wait("vector", tmm_done)
                for dc in range(DC):
                    b.do("vector", lambda e, tt=tt, dc=dc: e.scalar_tensor_tensor(
                        out=xv(dc, tt * 512, 512), in0=bank(dc),
                        scalar=vec8[:, VEC["b_f2b"] + dc:VEC["b_f2b"] + dc + 1],
                        in1=xv(dc, tt * 512, 512), op0=ALU.add, op1=ALU.add))
                tevx = b.last("vector", sV)
                b.wait("tensor", tevx)
            return tevx

        tchain = ts1
        if n_sub >= 2:
            tchain = emit_ffn(tchain, first_call=True)
        if n_sub >= 3:
            tchain = emit_s3(tchain)
        if n_sub >= 4:
            tchain = emit_ffn(tchain, first_call=False)
        return finish(nc, es, b, tchain)


def _prep_inputs(inputs):
    x = np.asarray(inputs["L0_attn_a_x"])
    Bb, T, _ = x.shape
    f32 = np.float32
    import ml_dtypes
    bf = ml_dtypes.bfloat16
    t = lambda a: np.ascontiguousarray(np.asarray(a, dtype=f32))
    r8 = lambda v: np.ascontiguousarray(np.asarray(v, f32).reshape(DC, 128).T)

    aw = {}
    aw["a_wshT"] = t(inputs["L0_attn_a_W_shift"].T)
    aw["a_wkT"] = t(inputs["L0_attn_a_W_key"].T)
    aw["a_wvT"] = t(inputs["L0_attn_a_W_value"].T)
    aw["a_woT"] = t(inputs["L0_attn_a_W_output"].T)
    aw["a_gate"] = r8(inputs["L0_attn_a_shift_gate"])
    aw["a_lnw"] = r8(inputs["L0_attn_a_ln_w"])
    aw["a_lnb"] = r8(inputs["L0_attn_a_ln_b"])
    aw["a_td"] = t(np.asarray(inputs["L0_attn_a_time_decay"]).reshape(S, 1))
    aw["a_tf"] = t(np.asarray(inputs["L0_attn_a_time_first"]).reshape(S, 1))
    aw["f_gT"] = t(inputs["L0_ffn_gate_w"].T)
    aw["f_wsin"] = t(inputs["L0_ffn_ws_in"])
    aw["f_vsin"] = t(inputs["L0_ffn_vs_in"])
    aw["f_wsout"] = t(inputs["L0_ffn_ws_out"])
    aw["f_ewin"] = t(np.asarray(inputs["L0_ffn_e_win"]).reshape(E * D, H))
    aw["f_evin"] = t(np.asarray(inputs["L0_ffn_e_vin"]).reshape(E * D, H))
    aw["f_ewout"] = t(np.asarray(inputs["L0_ffn_e_wout"]).reshape(E * H, D))
    aw["b_ln1w"] = r8(inputs["L0_attn_b_ln1_w"]); aw["b_ln1b"] = r8(inputs["L0_attn_b_ln1_b"])
    aw["b_ln2w"] = r8(inputs["L0_attn_b_ln2_w"]); aw["b_ln2b"] = r8(inputs["L0_attn_b_ln2_b"])
    aw["b_qT"] = t(inputs["L0_attn_b_q_w"].T)
    aw["b_qb"] = t(np.asarray(inputs["L0_attn_b_q_b"]).reshape(S, 1))
    aw["b_kT"] = t(inputs["L0_attn_b_k_w"].T)
    aw["b_kb"] = t(np.asarray(inputs["L0_attn_b_k_b"]).reshape(S, 1))
    aw["b_vT"] = t(inputs["L0_attn_b_v_w"].T)
    aw["b_vbrow"] = t(np.asarray(inputs["L0_attn_b_v_b"]).reshape(1, D))
    aw["b_oT"] = t(inputs["L0_attn_b_o_w"].T)
    aw["b_ob"] = r8(inputs["L0_attn_b_o_b"])
    aw["b_f1T"] = t(inputs["L0_attn_b_f1_w"].T)
    aw["b_f1b"] = np.ascontiguousarray(np.asarray(inputs["L0_attn_b_f1_b"], f32).reshape(32, 128).T)
    aw["b_f2T"] = t(inputs["L0_attn_b_f2_w"].T)
    aw["b_f2b"] = r8(inputs["L0_attn_b_f2_b"])
    aw["ident_in"] = np.eye(128, dtype=f32)
    aw["identb_in"] = np.eye(128, dtype=f32).astype(bf)
    trm = np.zeros((128, 128), dtype=f32)
    for j in range(128):
        trm[j, j:] = 1.0
    aw["tri_in"] = trm.astype(bf)
    aw["ones_f_in"] = np.ones((1, 128), f32)
    aw["ones_b_in"] = np.ones((1, 128), bf)
    aw["onesc_b_in"] = np.ones((128, 1), bf)
    aw["consts_in"] = np.broadcast_to(
        np.array([LN_EPS, ATTN_EPS, -0.5, -1.0, 1.0, 0.0, 0.5, 2.0], f32), (128, 8)).copy()
    ese = np.zeros((E, E * 128), bf)
    for e in range(E):
        ese[e, e * 128:(e + 1) * 128] = 1.0
    aw["esel_in"] = ese

    shift = T // 2
    per = []
    for ci in range(NCORE):
        bb, j = ci // 4, ci % 4
        t0 = j * TC
        own = np.ascontiguousarray(x[bb, t0:t0 + TC, :].T.astype(f32))
        s0 = (t0 + shift) % T
        sh = np.ascontiguousarray(x[bb, s0:s0 + TC, :].T.astype(f32))
        base = np.zeros((S, NCORE), f32)
        mask = np.zeros((S, NCORE), f32)
        kvm = np.zeros((S, NCORE), f32)
        for k in range(NCORE):
            kb_, kj = k // 4, k % 4
            if kb_ == bb and kj < j:
                base[:, k] = float(TC) * (j - 1 - kj)
                mask[:, k] = 1.0
                kvm[:, k] = 1.0
        ohc = np.zeros((S, NCORE), f32)
        ohc[:, ci] = 1.0
        im = dict(aw)
        im["ohc_in"] = ohc
        im["xin"] = own
        im["xsh_in"] = sh
        im["rwkv_base"] = base
        im["rwkv_mask"] = mask
        im["kv_maskin"] = kvm
        per.append(im)
    return per


def kernel(**inputs):
    per = _prep_inputs(inputs)
    nc = build_nc(_NSUB)
    res = run_bass_kernel_spmd(nc, per, core_ids=list(range(NCORE)))
    x = np.asarray(inputs["L0_attn_a_x"])
    Bb, T, Dm = x.shape
    out = np.empty((Bb, T, Dm), dtype=np.float32)
    for ci in range(NCORE):
        bb, j = ci // 4, ci % 4
        out[bb, j * TC:(j + 1) * TC, :] = res.results[ci]["xout"].T
    return out


# version 05 (final)
# speedup vs baseline: 1.1789x; 1.1789x over previous
import numpy as np
from contextlib import ExitStack

import concourse.bass as bass
import concourse.mybir as mybir
from concourse.bass_utils import run_bass_kernel_spmd

F32 = mybir.dt.float32
BF16 = mybir.dt.bfloat16
AF = mybir.ActivationFunctionType
ALU = mybir.AluOpType
AX = mybir.AxisListType

NCORE = 8
D = 1024
DC = 8
TC = 2048
S = 64
E = 16
H = 512
HS = 1024
F4 = 4096
CH = 128
NCH = TC // CH
SEG = 128
LN_EPS = 1e-5
ATTN_EPS = 1e-6
BIGNEG = -30000.0

_NSUB = 4
_STAGE = 99


class Sem:
    def __init__(self, es, nc, name):
        self.h = es.enter_context(nc.semaphore(name))
        self.n = 0


class B:
    ENGINES = ("sync", "vector", "scalar", "tensor", "gpsimd")

    def __init__(self, nc):
        self.nc = nc
        self.q = {e: [] for e in self.ENGINES}

    def do(self, eng, fn):
        self.q[eng].append(fn)

    def wait(self, eng, *toks):
        best = {}
        for t in toks:
            if t is None:
                continue
            s, v = t
            if v > 0:
                best[s.h.num] = (s, max(v, best.get(s.h.num, (s, 0))[1]))
        for s, v in best.values():
            self.do(eng, lambda e, s=s, v=v: e.wait_ge(s.h, v))

    def inc(self, eng, fn, sem, n=1):
        sem.n += n
        v = sem.n
        self.do(eng, lambda e, f=fn, s=sem, n=n: f(e).then_inc(s.h, n))
        return (sem, v)

    def dma(self, eng, out, in_, sem):
        return self.inc(eng, lambda e: e.dma_start(out=out, in_=in_), sem, 16)

    def last(self, eng, sem, n=1):
        fn = self.q[eng].pop()
        sem.n += n
        v = sem.n
        self.q[eng].append(lambda e, f=fn, s=sem, n=n: f(e).then_inc(s.h, n))
        return (sem, v)


def build_nc(n_sub=4):
    nc = bass.Bass("TRN2", target_bir_lowering=False)
    es = ExitStack()

    din = lambda name, shape, dt=F32: nc.dram_tensor(name, shape, dt, kind="ExternalInput").ap()
    dint = lambda name, shape, dt=F32: nc.dram_tensor(name, shape, dt, kind="Internal").ap()

    xin = din("xin", [D, TC])
    xsh_in = din("xsh_in", [D, TC])
    a_wshT = din("a_wshT", [D, D])
    a_wkT = din("a_wkT", [D, S])
    a_wvT = din("a_wvT", [D, S])
    a_woT = din("a_woT", [S, D])
    a_gate = din("a_gate", [128, DC])
    a_lnw = din("a_lnw", [128, DC])
    a_lnb = din("a_lnb", [128, DC])
    a_td = din("a_td", [S, 1])
    a_tf = din("a_tf", [S, 1])
    rwkv_base = din("rwkv_base", [S, NCORE])
    rwkv_mask = din("rwkv_mask", [S, NCORE])
    f_gT = din("f_gT", [D, E])
    f_wsin = din("f_wsin", [D, HS])
    f_vsin = din("f_vsin", [D, HS])
    f_wsout = din("f_wsout", [HS, D])
    f_ewin = din("f_ewin", [E * D, H])
    f_evin = din("f_evin", [E * D, H])
    f_ewout = din("f_ewout", [E * H, D])
    b_ln1w = din("b_ln1w", [128, DC]); b_ln1b = din("b_ln1b", [128, DC])
    b_ln2w = din("b_ln2w", [128, DC]); b_ln2b = din("b_ln2b", [128, DC])
    b_qT = din("b_qT", [D, S]); b_qb = din("b_qb", [S, 1])
    b_kT = din("b_kT", [D, S]); b_kb = din("b_kb", [S, 1])
    b_vT = din("b_vT", [D, D]); b_vbrow = din("b_vbrow", [1, D])
    b_oT = din("b_oT", [D, D]); b_ob = din("b_ob", [128, DC])
    b_f1T = din("b_f1T", [D, F4]); b_f1b = din("b_f1b", [128, 32])
    b_f2T = din("b_f2T", [F4, D]); b_f2b = din("b_f2b", [128, DC])
    kv_maskin = din("kv_maskin", [S, NCORE])
    ohc_in = din("ohc_in", [S, NCORE])
    ident_in = din("ident_in", [128, 128])
    identb_in = din("identb_in", [128, 128], BF16)
    tri_in = din("tri_in", [128, 128], BF16)
    ones_f_in = din("ones_f_in", [1, 128])
    ones_b_in = din("ones_b_in", [1, 128], BF16)
    onesc_b_in = din("onesc_b_in", [128, 1], BF16)
    esel_in = din("esel_in", [E, E * 128], BF16)
    consts_in = din("consts_in", [128, 8])

    xout = nc.dram_tensor("xout", [D, TC], F32, kind="ExternalOutput").ap()


    h_sc = dint("h_sc", [E * H + HS, TC], BF16)
    wout_sc = dint("wout_sc", [E * H + HS, D], BF16)
    vtm_sc = dint("vtm_sc", [TC, D], BF16)
    kvpref_sc = dint("kvpref_sc", [NCH * S, D], BF16)
    f2_sc = dint("f2_sc", [F4, D], BF16)
    h1_sc = dint("h1_sc", [F4, TC], BF16)
    z_sc = dint("z_sc", [D, TC], BF16)
    ag1_in = dint("ag1_in", [S, NCORE])
    ag1_out = dint("ag1_out", [S, NCORE])
    ag2_in = dint("ag2_in", [S, NCORE * 1040])
    ag2_out = dint("ag2_out", [S, NCORE * 1040])

    sb = lambda n, s, d=F32: es.enter_context(nc.sbuf_tensor(n, s, d))
    with es:
        x = sb("x", [128, DC * TC])
        B1 = sb("B1", [128, DC * TC], BF16)
        B2 = sb("B2", [128, 12288], BF16)
        SV = sb("SV", [128, 4096], BF16)
        SC = sb("SC", [128, 2048])
        SCB = sb("SCB", [128, 6144], BF16)
        PBK = sb("PBK", [S, 2 * TC], BF16)
        wkvb = PBK[0:S, 0:TC]
        states = PBK[0:S, TC:2 * TC]
        ksqk = sb("ksqk", [S, TC], BF16)
        qb16 = PBK[0:S, 0:TC]
        kb16 = PBK[0:S, TC:2 * TC]
        kvpref_b = sb("kvpref_b", [S, D], BF16)
        kvin_b = sb("kvin_b", [S, D + 16], BF16)
        kvrun = sb("kvrun", [S, D + 16])
        agkv = sb("agkv", [S, 528])
        esel = sb("esel", [E, E * 128], BF16)
        combfm_b = sb("combfm_b", [E, TC], BF16)
        gcomb = sb("gcomb", [128, 3 * E * NCH])
        stage = sb("stage", [128, 1024], BF16)
        cbe = SCB[:, 4096:5120]
        hstr = stage[:, 0:1024]
        ident = sb("ident", [128, 128])
        identb = sb("identb", [128, 128], BF16)
        tri = sb("tri", [128, 128], BF16)
        ones_f = sb("ones_f", [1, 128])
        ones_b = sb("ones_b", [1, 128], BF16)
        onesc_b = sb("onesc_b", [128, 1], BF16)
        vec8 = sb("vec8", [128, 80])
        v64 = sb("v64", [S, 64])
        tr64 = sb("tr64", [S, 16])
        m128 = sb("m128", [128, 48])
        f1b_sb = sb("f1b_sb", [128, 32])
        vbrow_b = sb("vbrow_b", [1, D], BF16)
        denloc = sb("denloc", [128, NCH])
        den1 = sb("den1", [128, 4])
        qpb = sb("qpb", [S, CH], BF16)
        ktm_b = sb("ktm_b", [CH, S], BF16)
        atile = sb("atile", [128, CH], BF16)
        rrow = sb("rrow", [1, 2 * CH])
        wful_t = sb("wful_t", [S, 2 * SEG])
        ag1f = sb("ag1f", [S, NCORE])
        consts = sb("consts", [128, 8])
        _cvals = [LN_EPS, ATTN_EPS, -0.5, -1.0, 1.0, 0.0, 0.5, 2.0]
        for _i, _v in enumerate(_cvals):
            nc.const_aps.register(F32, _v, consts[:, _i:_i + 1])

        PS = es.enter_context(nc.psum_tensor("PS", [128, 4096], F32))
        bank = lambda i, n=512: PS[:, i * 512:i * 512 + n]

        R = sb("R", [1, 2048])
        MU = R[0:1, 0:512]
        VAR = R[0:1, 512:1024]
        TROW = R[0:1, 1024:1536]
        ROW2 = R[0:1, 1536:2048]
        SBP = sb("SBP", [128, 2048], BF16)
        M2BP = sb("M2BP", [128, 2048], BF16)

        SB_ = SCB[:, 0:2048]
        M2B = SCB[:, 2048:4096]
        SQR = SCB[:, 4096:6144]

        sW = Sem(es, nc, "sW")
        sX = Sem(es, nc, "sX")
        sW2 = Sem(es, nc, "sW2")
        sQ = Sem(es, nc, "sQ")
        sStA = Sem(es, nc, "sStA")
        sStB = Sem(es, nc, "sStB")
        sG = Sem(es, nc, "sG")
        sV = Sem(es, nc, "sV")
        sA = Sem(es, nc, "sA")
        sT = Sem(es, nc, "sT")
        sC1 = Sem(es, nc, "sC1")
        sC2 = Sem(es, nc, "sC2")

        b = B(nc)

        def finish(nc_, es_, b_, tchain, **kw):
            b.wait("sync", tchain)
            sO = Sem(es, nc, "sO")
            tso = b.dma("sync", xout.rearrange("(dc p) t -> p dc t", p=128),
                        x[:].rearrange("p (dc t) -> p dc t", t=TC), sO)
            b.wait("sync", tso)
            b.wait("gpsimd", tso)
            b.wait("vector", tso)
            b.wait("scalar", tso)
            b.wait("tensor", tso)
            rem = nc.sbuf_bytes_remaining
            assert rem >= 0, f"SBUF over budget by {-rem}"
            blk = es.enter_context(nc.Block())
            blk.sync(lambda eng: [fn(eng) for fn in b.q["sync"]])
            blk.vector(lambda eng: [fn(eng) for fn in b.q["vector"]])
            blk.scalar(lambda eng: [fn(eng) for fn in b.q["scalar"]])
            blk.tensor(lambda eng: [fn(eng) for fn in b.q["tensor"]])
            blk.gpsimd(lambda eng: [fn(eng) for fn in b.q["gpsimd"]])
            es.close()
            return nc

        xv = lambda dc, t0, n: x[:, dc * TC + t0: dc * TC + t0 + n]
        b1v = lambda dc, t0, n: B1[:, dc * TC + t0: dc * TC + t0 + n]

        def emit_rows_bcast(tstat, width, sb_dst=None, m2_dst=None, post_rows=None):
            sb_dst = sb_dst if sb_dst is not None else SB_
            m2_dst = m2_dst if m2_dst is not None else M2B
            b.wait("vector", tstat)
            tb = None
            for tq in range(width // 512):
                s0 = tq * 512
                b.do("vector", lambda e, s0=s0: e.tensor_scalar(
                    out=MU, in0=PS[0:1, s0:s0 + 512], scalar1=1.0 / D, scalar2=None,
                    op0=ALU.mult))
                t_ = b.inc("vector", lambda e, s0=s0: e.tensor_scalar(
                    out=VAR, in0=PS[0:1, 2048 + s0:2048 + s0 + 512],
                    scalar1=1.0 / D, scalar2=None, op0=ALU.mult), sV, 1)
                b.wait("vector", t_)
                t_ = b.inc("vector", lambda e: e.tensor_tensor(out=TROW, in0=MU, in1=MU,
                                                               op=ALU.mult), sV, 1)
                b.wait("vector", t_)
                tv = b.inc("vector", lambda e: e.tensor_tensor(out=VAR, in0=VAR, in1=TROW,
                                                               op=ALU.subtract), sV, 1)
                b.wait("scalar", tv)
                tl_ = b.inc("scalar", lambda e: e.activation(out=VAR, in_=VAR, func=AF.Ln,
                                                             bias=LN_EPS), sA, 1)
                b.wait("scalar", tl_)
                tr = b.inc("scalar", lambda e: e.activation(out=VAR, in_=VAR, func=AF.Exp,
                                                            scale=-0.5), sA, 1)
                b.wait("vector", tr)
                tm2 = b.inc("vector", lambda e: e.tensor_tensor(out=TROW, in0=MU, in1=VAR,
                                                                op=ALU.mult), sV, 1)
                b.wait("vector", tm2)
                srow, m2row, textra = VAR, TROW, tm2
                if post_rows is not None:
                    srow, m2row, textra = post_rows(tq, tm2)
                b.wait("tensor", textra)
                b.do("tensor", lambda e, s0=s0, srow=srow: e.matmul(
                    out=PS[:, s0:s0 + 512], lhsT=ones_f[:], rhs=srow,
                    start=True, stop=True))
                b.do("tensor", lambda e, s0=s0, m2row=m2row: e.matmul(
                    out=PS[:, 2048 + s0:2048 + s0 + 512], lhsT=ones_f[:], rhs=m2row,
                    start=True, stop=True))
                tbm = b.last("tensor", sT)
                b.wait("vector", tbm)
                b.do("vector", lambda e, s0=s0: e.tensor_copy(
                    out=sb_dst[:, s0:s0 + 512], in_=PS[:, s0:s0 + 512]))
                tb = b.inc("vector", lambda e, s0=s0: e.tensor_copy(
                    out=m2_dst[:, s0:s0 + 512], in_=PS[:, 2048 + s0:2048 + s0 + 512]),
                    sV, 1)
                b.wait("tensor", tb)
            return tb

        def emit_ln_stats(src_bf_chunk, after=()):
            b.wait("scalar", *after)
            b.wait("tensor", *after)
            for dc in range(DC):
                tsq = b.inc("scalar", lambda e, dc=dc: e.activation(
                    out=SQR, in_=src_bf_chunk(dc), func=AF.Square), sA, 1)
                for tq in range(4):
                    b.do("tensor", lambda e, dc=dc, tq=tq: e.matmul(
                        out=PS[0:1, tq * 512:(tq + 1) * 512], lhsT=onesc_b[:],
                        rhs=src_bf_chunk(dc)[:, tq * 512:(tq + 1) * 512],
                        start=(dc == 0), stop=(dc == DC - 1)))
                b.wait("tensor", tsq)
                for tq in range(4):
                    b.do("tensor", lambda e, tq=tq, dc=dc: e.matmul(
                        out=PS[0:1, 2048 + tq * 512:2048 + (tq + 1) * 512], lhsT=onesc_b[:],
                        rhs=SQR[:, tq * 512:(tq + 1) * 512],
                        start=(dc == 0), stop=(dc == DC - 1)))
                tmm = b.last("tensor", sT)
                if dc < DC - 1:
                    b.wait("scalar", tmm)
            return emit_rows_bcast(tmm, 2048)

        def emit_norm(src_chunk, dst_chunk, lnw=None, lnwoff=0, after=()):
            b.wait("vector", *after)
            for dc in range(DC):
                b.do("vector", lambda e, dc=dc: e.tensor_tensor(
                    out=dst_chunk(dc), in0=src_chunk(dc), in1=SB_, op=ALU.mult))
            t1 = b.last("vector", sV)
            b.wait("vector", t1)
            for dc in range(DC):
                b.do("vector", lambda e, dc=dc: e.tensor_tensor(
                    out=dst_chunk(dc), in0=dst_chunk(dc), in1=M2B, op=ALU.subtract))
            tn = b.last("vector", sV)
            if lnw is not None:
                b.wait("vector", tn)
                for dc in range(DC):
                    b.do("vector", lambda e, dc=dc: e.tensor_scalar(
                        out=dst_chunk(dc), in0=dst_chunk(dc),
                        scalar1=vec8[:, lnwoff + dc:lnwoff + dc + 1],
                        scalar2=vec8[:, lnwoff + DC + dc:lnwoff + DC + dc + 1],
                        op0=ALU.mult, op1=ALU.add))
                tn = b.last("vector", sV)
            return tn

        def emit_scan(out_fn, d0_ap, d1_fn, n, extra_wait=None, init0=0.0):
            b.wait("vector", extra_wait)
            last = None
            for s0 in range(0, n, SEG):
                init = init0 if s0 == 0 else out_fn(s0 - 1, 1)
                last = b.inc("vector", lambda e, s0=s0, init=init: e.tensor_tensor_scan(
                    out=out_fn(s0, SEG), data0=d0_ap, data1=d1_fn(s0, SEG),
                    initial=init, op0=ALU.mult, op1=ALU.add), sV, 1)
                b.wait("vector", last)
            return last

        for ap_s, ap_d in ((ident, ident_in), (identb, identb_in), (tri, tri_in),
                           (ones_f, ones_f_in), (ones_b, ones_b_in),
                           (onesc_b, onesc_b_in), (esel, esel_in), (f1b_sb, b_f1b),
                           (consts, consts_in)):
            b.dma("sync", ap_s[:], ap_d, sW)
        VEC = {}
        voff = 0
        for name, ap_d in (("a_gate", a_gate), ("a_lnw", a_lnw), ("a_lnb", a_lnb),
                           ("b_ln1w", b_ln1w), ("b_ln1b", b_ln1b),
                           ("b_ln2w", b_ln2w), ("b_ln2b", b_ln2b),
                           ("b_ob", b_ob), ("b_f2b", b_f2b)):
            b.dma("sync", vec8[:, voff:voff + DC], ap_d, sW)
            VEC[name] = voff
            voff += DC
        V64 = {}
        v64off = 0
        for name, ap_d, w in (("a_td", a_td, 1), ("a_tf", a_tf, 1),
                              ("rwkv_base", rwkv_base, NCORE),
                              ("rwkv_mask", rwkv_mask, NCORE),
                              ("kv_mask", kv_maskin, NCORE),
                              ("ohc", ohc_in, NCORE),
                              ("b_qb", b_qb, 1), ("b_kb", b_kb, 1)):
            b.dma("sync", v64[:, v64off:v64off + w], ap_d, sW)
            V64[name] = v64off
            v64off += w
        v64c = lambda nm: v64[:, V64[nm]:V64[nm] + 1]
        tsm = (sW, sW.n)
        tx = b.dma("sync", x[:].rearrange("p (dc t) -> p dc t", t=TC),
                   xin.rearrange("(dc p) t -> p dc t", p=128), sX)

        b.wait("scalar", tsm)
        tg = b.inc("scalar", lambda e: e.activation(
            out=m128[:, 0:DC], in_=vec8[:, VEC["a_gate"]:VEC["a_gate"] + DC],
            func=AF.Sigmoid), sA, 1)
        b.wait("vector", tg)
        tgn = b.inc("vector", lambda e: e.tensor_scalar(
            out=m128[:, 8:16], in0=m128[:, 0:DC], scalar1=-1.0, scalar2=None,
            op0=ALU.mult), sV, 1)

        wsh_r = a_wshT.rearrange("(dk p) m -> p dk m", p=128)
        tcast = []
        for pc in range(8):
            half = SC[:, (pc % 2) * 1024:(pc % 2) * 1024 + 1024]
            if pc >= 2:
                b.wait("sync", tcast[pc - 2])
            tld = b.dma("sync", half, wsh_r[:, pc:pc + 1, :], sStA if pc % 2 == 0 else sStB)
            b.wait("vector", tld)
            tcast.append(b.inc("vector", lambda e, half=half, pc=pc: e.tensor_copy(
                out=B2[:, pc * 1024:(pc + 1) * 1024], in_=half), sV, 1))
        twsh = tcast[-1]
        wsh_t = lambda dk, mc: B2[:, dk * 1024 + mc * 128: dk * 1024 + (mc + 1) * 128]
        b.wait("sync", tcast[-2], tcast[-1])
        t1 = b.dma("sync", SC[:, 0:512].rearrange("p (dk m) -> p dk m", m=S),
                   a_wkT.rearrange("(dk p) m -> p dk m", p=128), sW2)
        t2 = b.dma("sync", SC[:, 512:1024].rearrange("p (dk m) -> p dk m", m=S),
                   a_wvT.rearrange("(dk p) m -> p dk m", p=128), sW2)
        t3 = b.dma("sync", SC[0:S, 1024:2048], a_woT, sW2)
        b.wait("vector", t1, t2, t3)
        b.do("vector", lambda e: e.tensor_copy(out=SV[:, 0:512], in_=SC[:, 0:512]))
        b.do("vector", lambda e: e.tensor_copy(out=SV[:, 512:1024], in_=SC[:, 512:1024]))
        tkvw = b.inc("vector", lambda e: e.tensor_copy(out=SV[0:S, 1024:2048],
                                                       in_=SC[0:S, 1024:2048]), sV, 1)
        wk_t = lambda dk: SV[:, dk * S:(dk + 1) * S]
        wv_t = lambda dk: SV[:, 512 + dk * S: 512 + (dk + 1) * S]
        wo_t = lambda mc: SV[0:S, 1024 + mc * 128: 1024 + (mc + 1) * 128]

        def xroll(dc):
            return stage[:, 0:2048]

        def emit_ln_stats_rolled(src_f32_chunk, after=(), sb_dst=None, m2_dst=None,
                                 post_rows=None):
            tprev = None
            b.wait("vector", *after)
            b.wait("tensor", *after)
            for dc in range(DC):
                if tprev is not None:
                    b.wait("vector", tprev)
                tc_ = b.inc("vector", lambda e, dc=dc: e.tensor_copy(
                    out=SCB[:, 0:2048], in_=src_f32_chunk(dc)), sV, 1)
                b.wait("scalar", tc_)
                tsq = b.inc("scalar", lambda e: e.activation(
                    out=SQR, in_=SCB[:, 0:2048], func=AF.Square), sA, 1)
                b.wait("tensor", tc_)
                for tq in range(4):
                    b.do("tensor", lambda e, tq=tq, dc=dc: e.matmul(
                        out=PS[0:1, tq * 512:(tq + 1) * 512], lhsT=onesc_b[:],
                        rhs=SCB[:, tq * 512:(tq + 1) * 512],
                        start=(dc == 0), stop=(dc == DC - 1)))
                b.wait("tensor", tsq)
                for tq in range(4):
                    b.do("tensor", lambda e, tq=tq, dc=dc: e.matmul(
                        out=PS[0:1, 2048 + tq * 512:2048 + (tq + 1) * 512], lhsT=onesc_b[:],
                        rhs=SQR[:, tq * 512:(tq + 1) * 512],
                        start=(dc == 0), stop=(dc == DC - 1)))
                tprev = b.last("tensor", sT)
                b.wait("scalar", tprev)
            return emit_rows_bcast(tprev, 2048, sb_dst=sb_dst, m2_dst=m2_dst,
                                   post_rows=post_rows)

        if n_sub == 0 or _STAGE < 1:
            return finish(nc, es, b, tx)
        tb1 = emit_ln_stats_rolled(lambda dc: xv(dc, 0, TC), after=[tx, tsm])
        if _STAGE < 1.5:
            return finish(nc, es, b, tb1)
        b.wait("vector", tb1, tgn)
        for dc in range(DC):
            b.do("vector", lambda e, dc=dc: e.tensor_tensor(
                out=b1v(dc, 0, TC), in0=xv(dc, 0, TC), in1=SB_, op=ALU.mult))
        t_ = b.last("vector", sV); b.wait("vector", t_)
        for dc in range(DC):
            b.do("vector", lambda e, dc=dc: e.tensor_tensor(
                out=b1v(dc, 0, TC), in0=b1v(dc, 0, TC), in1=M2B, op=ALU.subtract))
        t_ = b.last("vector", sV); b.wait("vector", t_)
        for dc in range(DC):
            b.do("vector", lambda e, dc=dc: e.scalar_tensor_tensor(
                out=b1v(dc, 0, TC), in0=b1v(dc, 0, TC), scalar=m128[:, 8 + dc:9 + dc],
                in1=b1v(dc, 0, TC), op0=ALU.mult, op1=ALU.add))
        ty1g = b.last("vector", sV)

        if _STAGE < 2:
            return finish(nc, es, b, ty1g)
        def xshq_dram(tq, grp):
            return xsh_in.rearrange("(dc p) t -> p dc t", p=128)[
                :, grp * 4:(grp + 1) * 4, tq * 512:(tq + 1) * 512]

        scq4 = lambda dc4: SC[:, dc4 * 512:(dc4 + 1) * 512]
        yshg = lambda dc4: SV[:, 2048 + dc4 * 512:2048 + (dc4 + 1) * 512]
        tevac_q = None

        def qstats(grp, first, last_tok):
            tprev = last_tok
            for dc4 in range(4):
                dc = grp * 4 + dc4
                if tprev is not None:
                    b.wait("vector", tprev)
                tc_ = b.inc("vector", lambda e, dc4=dc4: e.tensor_copy(
                    out=stage[:, 0:512], in_=scq4(dc4)), sV, 1)
                b.wait("scalar", tc_)
                tsq = b.inc("scalar", lambda e: e.activation(
                    out=SQR[:, 0:512], in_=stage[:, 0:512], func=AF.Square), sA, 1)
                b.wait("tensor", tc_)
                b.do("tensor", lambda e, dc=dc, first=first: e.matmul(
                    out=PS[0:1, 0:512], lhsT=onesc_b[:], rhs=stage[:, 0:512],
                    start=(first and dc % 4 == 0 and dc < 4) or (dc == 0),
                    stop=(dc == DC - 1)))
                b.wait("tensor", tsq)
                b.do("tensor", lambda e, dc=dc: e.matmul(
                    out=PS[0:1, 512:1024], lhsT=onesc_b[:], rhs=SQR[:, 0:512],
                    start=(dc == 0), stop=(dc == DC - 1)))
                tprev = b.last("tensor", sT)
                b.wait("scalar", tprev)
            return tprev

        for tq in range(4):
            if tevac_q is not None:
                b.wait("sync", tevac_q)
            else:
                b.wait("sync", tkvw)
            tld = b.dma("sync", SC[:, 0:2048].rearrange("p (c t) -> p c t", t=512),
                        xshq_dram(tq, 0), sQ)
            b.wait("vector", tld)
            b.wait("tensor", tld)
            t_s0 = qstats(0, True, None)
            b.wait("sync", t_s0)
            tld1 = b.dma("sync", SC[:, 0:2048].rearrange("p (c t) -> p c t", t=512),
                         xshq_dram(tq, 1), sQ)
            b.wait("vector", tld1)
            b.wait("tensor", tld1)
            t_s1 = qstats(1, False, tld1)
            b.wait("vector", t_s1)
            b.do("vector", lambda e: e.tensor_scalar(out=MU, in0=PS[0:1, 0:512],
                                                     scalar1=1.0 / D, scalar2=None,
                                                     op0=ALU.mult))
            t_ = b.inc("vector", lambda e: e.tensor_scalar(out=VAR, in0=PS[0:1, 512:1024],
                                                           scalar1=1.0 / D, scalar2=None,
                                                           op0=ALU.mult), sV, 1)
            b.wait("vector", t_)
            t_ = b.inc("vector", lambda e: e.tensor_tensor(out=TROW, in0=MU, in1=MU,
                                                           op=ALU.mult), sV, 1)
            b.wait("vector", t_)
            tv = b.inc("vector", lambda e: e.tensor_tensor(out=VAR, in0=VAR, in1=TROW,
                                                           op=ALU.subtract), sV, 1)
            b.wait("scalar", tv)
            tl_ = b.inc("scalar", lambda e: e.activation(out=VAR, in_=VAR, func=AF.Ln,
                                                         bias=LN_EPS), sA, 1)
            b.wait("scalar", tl_)
            trq = b.inc("scalar", lambda e: e.activation(out=VAR, in_=VAR, func=AF.Exp,
                                                         scale=-0.5), sA, 1)
            b.wait("vector", trq)
            tm2q = b.inc("vector", lambda e: e.tensor_tensor(out=TROW, in0=MU, in1=VAR,
                                                             op=ALU.mult), sV, 1)
            b.wait("vector", tm2q)
            b.wait("tensor", tm2q)
            b.do("tensor", lambda e: e.matmul(out=PS[:, 1024:1536], lhsT=ones_f[:],
                                              rhs=VAR, start=True, stop=True))
            b.do("tensor", lambda e: e.matmul(out=PS[:, 1536:2048], lhsT=ones_f[:],
                                              rhs=TROW, start=True, stop=True))
            tbq = b.last("tensor", sT)
            b.wait("vector", tbq)
            b.do("vector", lambda e: e.tensor_copy(out=SB_[:, 0:512], in_=PS[:, 1024:1536]))
            tbc = b.inc("vector", lambda e: e.tensor_copy(out=M2B[:, 0:512],
                                                          in_=PS[:, 1536:2048]), sV, 1)
            b.wait("vector", tbc)
            for dc4 in range(4):
                b.do("vector", lambda e, dc4=dc4: e.tensor_tensor(
                    out=yshg(dc4), in0=scq4(dc4), in1=SB_[:, 0:512], op=ALU.mult))
            t_ = b.last("vector", sV); b.wait("vector", t_)
            for dc4 in range(4):
                b.do("vector", lambda e, dc4=dc4: e.tensor_tensor(
                    out=yshg(dc4), in0=yshg(dc4), in1=M2B[:, 0:512], op=ALU.subtract))
            tn1 = b.last("vector", sV)
            b.wait("tensor", tn1)
            if tq == 0:
                b.wait("tensor", twsh, ty1g)
            for dc in range(DC):
                for dk4 in range(4):
                    b.do("tensor", lambda e, dc=dc, dk4=dk4: e.matmul(
                        out=bank(dc), lhsT=wsh_t(4 + dk4, dc), rhs=yshg(dk4),
                        start=(dk4 == 0), stop=False))
            tmm1 = b.last("tensor", sT)
            b.wait("sync", tn1)
            tld0 = b.dma("sync", SC[:, 0:2048].rearrange("p (c t) -> p c t", t=512),
                         xshq_dram(tq, 0), sQ)
            b.wait("vector", tld0, tmm1)
            for dc4 in range(4):
                b.do("vector", lambda e, dc4=dc4: e.tensor_tensor(
                    out=yshg(dc4), in0=scq4(dc4), in1=SB_[:, 0:512], op=ALU.mult))
            t_ = b.last("vector", sV); b.wait("vector", t_)
            for dc4 in range(4):
                b.do("vector", lambda e, dc4=dc4: e.tensor_tensor(
                    out=yshg(dc4), in0=yshg(dc4), in1=M2B[:, 0:512], op=ALU.subtract))
            tn0 = b.last("vector", sV)
            b.wait("tensor", tn0)
            for dc in range(DC):
                for dk4 in range(4):
                    b.do("tensor", lambda e, dc=dc, dk4=dk4: e.matmul(
                        out=bank(dc), lhsT=wsh_t(dk4, dc), rhs=yshg(dk4),
                        start=False, stop=(dk4 == 3)))
            tmm = b.last("tensor", sT)
            b.wait("vector", tmm)
            for dc in range(DC):
                b.do("vector", lambda e, tq=tq, dc=dc: e.scalar_tensor_tensor(
                    out=b1v(dc, tq * 512, 512), in0=bank(dc), scalar=m128[:, dc:dc + 1],
                    in1=b1v(dc, tq * 512, 512), op0=ALU.mult, op1=ALU.add))
            tevac_q = b.last("vector", sV)
            b.wait("tensor", tevac_q)
        txshift = tevac_q

        if _STAGE < 3:
            return finish(nc, es, b, txshift)
        tb2 = emit_ln_stats(lambda dc: b1v(dc, 0, TC), after=[txshift])
        txn = emit_norm(lambda dc: b1v(dc, 0, TC), lambda dc: b1v(dc, 0, TC),
                        lnw=True, lnwoff=VEC["a_lnw"], after=[tb2])

        if _STAGE < 4:
            return finish(nc, es, b, txn)
        b.wait("tensor", txn, tkvw)
        for tq in range(4):
            for dk in range(DC):
                b.do("tensor", lambda e, tq=tq, dk=dk: e.matmul(
                    out=PS[0:S, tq * 512:(tq + 1) * 512], lhsT=wk_t(dk),
                    rhs=b1v(dk, tq * 512, 512), start=(dk == 0), stop=(dk == DC - 1)))
        for tq in range(4):
            for dk in range(DC):
                b.do("tensor", lambda e, tq=tq, dk=dk: e.matmul(
                    out=PS[0:S, 2048 + tq * 512:2048 + (tq + 1) * 512], lhsT=wv_t(dk),
                    rhs=b1v(dk, tq * 512, 512), start=(dk == 0), stop=(dk == DC - 1)))
        tkv = b.last("tensor", sT)
        b.wait("scalar", tkv)
        b.do("scalar", lambda e: e.activation(out=v64[:, 62:63], in_=v64c("a_tf"),
                                              func=AF.Exp))
        tsg = b.inc("scalar", lambda e: e.activation(out=SC[0:S, 0:2048],
                                                     in_=PS[0:S, 0:2048],
                                                     func=AF.Sigmoid), sA, 1)
        b.wait("vector", tsg)
        tt1 = b.inc("vector", lambda e: e.tensor_scalar(
            out=SC[0:S, 0:2048], in0=SC[0:S, 0:2048], scalar1=v64[:, 62:63],
            scalar2=None, op0=ALU.mult), sV, 1)
        b.wait("scalar", tt1)
        te1 = b.inc("scalar", lambda e: e.activation(out=SC[0:S, 0:2048],
                                                     in_=SC[0:S, 0:2048], func=AF.Exp,
                                                     scale=-1.0), sA, 1)
        b.wait("vector", te1)
        twkv = b.inc("vector", lambda e: e.tensor_tensor(
            out=PBK[0:S, 0:2048], in0=SC[0:S, 0:2048], in1=PS[0:S, 2048:4096],
            op=ALU.mult), sV, 1)

        b.do("scalar", lambda e: e.activation(out=v64[:, 61:62], in_=v64c("a_td"),
                                              func=AF.Exp))
        tdw = b.last("scalar", sA)
        wful = wful_t[0:S, 0:SEG]
        zful = wful_t[0:S, SEG:2 * SEG]
        b.wait("vector", tdw, twkv)
        b.do("vector", lambda e: e.memset(wful, 1.0))
        t_ = b.inc("vector", lambda e: e.memset(zful, 0.0), sV, 1)
        b.wait("vector", t_)
        twf = b.inc("vector", lambda e: e.tensor_scalar(out=wful, in0=wful,
                                                        scalar1=v64[:, 61:62],
                                                        scalar2=None, op0=ALU.mult), sV, 1)
        b.wait("vector", twf)
        tscan = emit_scan(lambda c0, n: states[:, c0:c0 + n], wful,
                          lambda c0, n: wkvb[:, c0:c0 + n], TC)

        if _STAGE < 5:
            return finish(nc, es, b, tscan)
        b.wait("vector", tscan)
        tf32 = b.inc("vector", lambda e: e.tensor_copy(out=tr64[:, 2:3],
                                                       in_=states[:, TC - 1:TC]), sV, 1)
        b.wait("vector", tf32)
        tarin = b.inc("vector", lambda e: e.tensor_scalar(
            out=ag1f[0:S, 0:NCORE], in0=v64[:, V64["ohc"]:V64["ohc"] + NCORE],
            scalar1=tr64[:, 2:3], scalar2=None, op0=ALU.mult), sV, 1)
        b.wait("gpsimd", tarin)
        tfd = b.dma("gpsimd", ag1_in, ag1f[0:S, 0:NCORE], sG)
        b.wait("gpsimd", tfd)
        tcc1 = b.inc("gpsimd", lambda e: e.collective_compute(
            "AllReduce", ALU.add, replica_groups=[list(range(NCORE))],
            ins=[ag1_in], outs=[ag1_out]), sC1, 1)
        b.wait("gpsimd", tcc1)
        tags = b.dma("gpsimd", tr64[:, 0:8], ag1_out, sG)
        b.wait("vector", tags)
        t_ = (sV, sV.n)
        tc1 = b.inc("vector", lambda e: e.tensor_scalar(
            out=tr64[:, 8:16], in0=v64[:, V64["rwkv_base"]:V64["rwkv_base"] + NCORE],
            scalar1=v64c("a_td"), scalar2=None, op0=ALU.mult), sV, 1)
        b.wait("scalar", tc1)
        tc2 = b.inc("scalar", lambda e: e.activation(out=tr64[:, 8:16], in_=tr64[:, 8:16],
                                                     func=AF.Exp), sA, 1)
        b.wait("vector", tc2, t_)
        t_ = b.inc("vector", lambda e: e.tensor_tensor(
            out=tr64[:, 8:16], in0=tr64[:, 8:16],
            in1=v64[:, V64["rwkv_mask"]:V64["rwkv_mask"] + NCORE], op=ALU.mult), sV, 1)
        b.wait("vector", t_)
        t_ = b.inc("vector", lambda e: e.tensor_tensor(out=tr64[:, 8:16], in0=tr64[:, 8:16],
                                                       in1=tr64[:, 0:8], op=ALU.mult), sV, 1)
        b.wait("vector", t_)
        tsin = b.inc("vector", lambda e: e.tensor_reduce(
            out=tr64[:, 1:2], in_=tr64[:, 8:16], axis=AX.X, op=ALU.add), sV, 1)
        b.wait("vector", tsin)
        twp = emit_scan(lambda c0, n: SC[0:S, c0:c0 + n], wful,
                        lambda c0, n: zful[:, 0:n], TC, init0=1.0)
        b.wait("vector", twp)
        tst = b.inc("vector", lambda e: e.scalar_tensor_tensor(
            out=states[:], in0=SC[0:S, 0:2048], scalar=tr64[:, 1:2], in1=states[:],
            op0=ALU.mult, op1=ALU.add), sV, 1)
        states_b = states

        if _STAGE < 6:
            return finish(nc, es, b, tst)
        b.wait("tensor", tst)
        tev = None
        for tq in range(4):
            for dc in range(DC):
                b.do("tensor", lambda e, tq=tq, dc=dc: e.matmul(
                    out=bank(dc), lhsT=wo_t(dc), rhs=states_b[:, tq * 512:(tq + 1) * 512],
                    start=True, stop=True))
            tmo = b.last("tensor", sT)
            b.wait("vector", tmo)
            for dc in range(DC):
                b.do("vector", lambda e, tq=tq, dc=dc: e.tensor_tensor(
                    out=xv(dc, tq * 512, 512), in0=bank(dc), in1=xv(dc, tq * 512, 512),
                    op=ALU.add))
            tev = b.last("vector", sV)
            b.wait("tensor", tev)
        ts1 = tev


        sH = Sem(es, nc, "sH")
        sHo = [Sem(es, nc, f"sHo{i}") for i in range(2)]
        sWoD = Sem(es, nc, "sWoD")
        sHsl = [Sem(es, nc, f"sHsl{i}") for i in range(8)]
        sWA = Sem(es, nc, "sWA")
        sWB = Sem(es, nc, "sWB")
        NU = E + 2
        WSL = [[0, 1, 2], [2, 0, 1], [1, 2, 0]]
        b2s = lambda sl, c0, n: B2[:, sl * 4096 + c0: sl * 4096 + c0 + n]
        VCX = SCB[:, 5248:5760]
        WOC = SCB[:, 5760:6016]

        def unit_dram(u):
            if u < E:
                return (f_ewin[u * D:(u + 1) * D, :], f_evin[u * D:(u + 1) * D, :],
                        f_ewout[u * H:(u + 1) * H, :], True)
            sub = u - E
            return (f_wsin[:, sub * H:(sub + 1) * H], f_vsin[:, sub * H:(sub + 1) * H],
                    f_wsout[sub * H:(sub + 1) * H, :], False)

        st_state = {"i": 0, "cons": {0: None, 1: None}}

        def stage_load(dram_piece, dst_bf, tgate=None):
            i = st_state["i"] % 2
            st_state["i"] += 1
            free = 1
            for dsz in dram_piece.shape[1:]:
                free *= dsz
            slot = SC[0:dram_piece.shape[0], i * 1024:i * 1024 + free]
            slot_in = slot
            if len(dram_piece.shape) == 3:
                slot_in = slot.rearrange("p (a m) -> p a m", m=dram_piece.shape[-1])
            if st_state["cons"][i] is not None:
                b.wait("sync", st_state["cons"][i])
            if tgate is not None:
                b.wait("sync", tgate)
            tld = b.dma("sync", slot_in, dram_piece, sStA if i == 0 else sStB)
            b.wait("scalar", tld)
            tc = b.inc("scalar", lambda e, slot=slot, dst_bf=dst_bf: e.activation(
                out=dst_bf, in_=slot, func=AF.Copy), sA, 1)
            st_state["cons"][i] = tc
            return tc

        def load_unit_w(u, which, slot, tgate=None):
            wdram = unit_dram(u)[which]
            w3 = wdram.rearrange("(dk p) m -> p dk m", p=128)
            toks = []
            for pc in range(4):
                toks.append(stage_load(w3[:, 2 * pc:2 * pc + 2, :],
                                       b2s(slot, pc * 1024, 1024), tgate))
            return toks[-1]

        def emit_ffn(tprev, first_call):
            tb = emit_ln_stats_rolled(lambda dc: xv(dc, 0, TC), after=[tprev])
            txn = emit_norm(lambda dc: xv(dc, 0, TC), lambda dc: b1v(dc, 0, TC),
                            after=[tb])
            tldg = stage_load(f_gT.rearrange("(dk p) m -> p dk m", p=128),
                              SCB[:, 5120:5248], tgate=tprev)
            gw_t = lambda dk: SCB[:, 5120 + dk * E:5120 + (dk + 1) * E]
            b.wait("tensor", tldg, txn)
            for tq in range(4):
                for dk in range(DC):
                    b.do("tensor", lambda e, tq=tq, dk=dk: e.matmul(
                        out=PS[0:E, tq * 512:(tq + 1) * 512], lhsT=gw_t(dk),
                        rhs=b1v(dk, tq * 512, 512), start=(dk == 0), stop=(dk == DC - 1)))
            tgm = b.last("tensor", sT)
            b.wait("scalar", tgm)
            tsig = b.inc("scalar", lambda e: e.activation(out=SC[0:E, 0:TC],
                                                          in_=PS[0:E, 0:TC],
                                                          func=AF.Sigmoid), sA, 1)
            st_state["cons"][0] = tsig
            st_state["cons"][1] = tsig
            b.wait("tensor", tsig)
            tev = None
            for c in range(NCH):
                po = 2048 + (c % 2) * 512
                if tev is not None:
                    b.wait("tensor", tev)
                ttr = b.inc("tensor", lambda e, c=c, po=po: e.transpose(
                    out=PS[0:128, po:po + 16], in_=SC[0:E, c * 128:(c + 1) * 128],
                    identity=ident[0:E, 0:E]), sT, 1)
                b.wait("vector", ttr)
                tev = b.inc("vector", lambda e, c=c, po=po: e.tensor_copy(
                    out=gcomb[:, c * 16:(c + 1) * 16], in_=PS[0:128, po:po + 16]), sV, 1)
            tgtr = (sT, sT.n)
            st_state["cons"][0] = tgtr
            st_state["cons"][1] = tgtr
            GA = lambda c: gcomb[:, c * 16:(c + 1) * 16]
            CBc = lambda c: gcomb[:, 256 + c * 16:256 + (c + 1) * 16]
            MSc = lambda c: gcomb[:, 512 + c * 16:512 + (c + 1) * 16]
            mk = lambda c: m128[:, c:c + 1]
            s4 = lambda c: m128[:, 16 + c:17 + c]
            b.wait("vector", tev)
            t_ = b.inc("vector", lambda e: e.tensor_copy(out=gcomb[:, 256:512],
                                                         in_=gcomb[:, 0:256]), sV, 1)
            b.wait("vector", t_)
            t_ = b.inc("vector", lambda e: e.memset(m128[:, 16:32], 0.0), sV, 1)
            b.wait("vector", t_)
            for kk in range(4):
                for c in range(NCH):
                    b.do("vector", lambda e, c=c: e.tensor_reduce(
                        out=mk(c), in_=CBc(c), axis=AX.X, op=ALU.max))
                t_ = b.last("vector", sV); b.wait("vector", t_)
                for c in range(NCH):
                    b.do("vector", lambda e, c=c: e.tensor_tensor(
                        out=s4(c), in0=s4(c), in1=mk(c), op=ALU.add))
                t_ = b.last("vector", sV); b.wait("vector", t_)
                if kk < 3:
                    for c in range(NCH):
                        b.do("vector", lambda e, c=c: e.tensor_scalar(
                            out=MSc(c), in0=CBc(c), scalar1=mk(c), scalar2=None,
                            op0=ALU.is_ge))
                    t_ = b.last("vector", sV); b.wait("vector", t_)
                    for c in range(NCH):
                        b.do("vector", lambda e, c=c: e.scalar_tensor_tensor(
                            out=CBc(c), in0=MSc(c), scalar=BIGNEG, in1=CBc(c),
                            op0=ALU.mult, op1=ALU.add))
                    t_ = b.last("vector", sV); b.wait("vector", t_)
            for c in range(NCH):
                b.do("vector", lambda e, c=c: e.tensor_scalar(
                    out=MSc(c), in0=GA(c), scalar1=mk(c), scalar2=None, op0=ALU.is_ge))
            t_ = b.last("vector", sV); b.wait("vector", t_)
            for c in range(NCH):
                b.do("vector", lambda e, c=c: e.tensor_tensor(
                    out=CBc(c), in0=GA(c), in1=MSc(c), op=ALU.mult))
            t_ = b.last("vector", sV); b.wait("vector", t_)
            t_ = b.inc("vector", lambda e: e.reciprocal(out=m128[:, 16:32],
                                                        in_=m128[:, 16:32]), sV, 1)
            b.wait("vector", t_)
            for c in range(NCH):
                b.do("vector", lambda e, c=c: e.tensor_scalar(
                    out=CBc(c), in0=CBc(c), scalar1=s4(c), scalar2=None, op0=ALU.mult))
            tcomb = b.last("vector", sV)
            b.wait("tensor", tcomb)
            tevf = None
            for c in range(NCH):
                po = 2048 + (c % 2) * 512
                if tevf is not None:
                    b.wait("tensor", tevf)
                ttr = b.inc("tensor", lambda e, c=c, po=po: e.transpose(
                    out=PS[0:E, po:po + 128], in_=CBc(c), identity=ident[:]), sT, 1)
                b.wait("vector", ttr)
                tevf = b.inc("vector", lambda e, c=c, po=po: e.tensor_copy(
                    out=combfm_b[:, c * 128:(c + 1) * 128], in_=PS[0:E, po:po + 128]),
                    sV, 1)
            tcfm = tevf

            tq_done = [None] * 8
            stg_done = [None, None]
            hml = [None]
            tmm_hin = None
            tunit_done = tcfm
            t_b2 = {}
            t_b2[WSL[0][0]] = load_unit_w(0, 0, WSL[0][0], tgate=tprev)
            t_b2[WSL[0][1]] = load_unit_w(0, 1, WSL[0][1], tgate=tprev)
            woc_tok = [None]

            def cast_wout(u):
                wodram = unit_dram(u)[2]
                wo3 = wodram.rearrange("(r p) m -> p r m", p=128)
                for pc in range(4):
                    i = st_state["i"] % 2
                    st_state["i"] += 1
                    slot = SC[:, i * 1024:i * 1024 + 1024]
                    if st_state["cons"][i] is not None:
                        b.wait("sync", st_state["cons"][i])
                    tld = b.dma("sync", slot, wo3[:, pc:pc + 1, :], sStA if i == 0 else sStB)
                    b.wait("scalar", tld)
                    if woc_tok[0] is not None:
                        b.wait("scalar", woc_tok[0])
                    for hh in range(4):
                        tc = b.inc("scalar", lambda e, slot=slot, hh=hh: e.activation(
                            out=WOC, in_=slot[:, hh * 256:(hh + 1) * 256], func=AF.Copy), sA, 1)
                        b.wait("gpsimd", tc)
                        tw = b.dma("gpsimd",
                                   wout_sc[u * H + pc * 128: u * H + (pc + 1) * 128,
                                           hh * 256:(hh + 1) * 256], WOC, sWoD)
                        b.wait("scalar", tw)
                        woc_tok[0] = tc
                    st_state["cons"][i] = tc

            for u in range(NU):
                Au, Bu, Pu = WSL[u % 3]
                wdram, vdram, wodram, gated = unit_dram(u)
                if u + 1 < NU:
                    t_b2[Pu] = load_unit_w(u + 1, 0, Pu, tgate=tunit_done)
                b.wait("tensor", t_b2[Au])
                thin_done = None
                thin_mm = None
                tsv_map = {}
                for th in range(2):
                    if gated:
                        b.wait("tensor", tcfm)
                        if tq_done[6] is not None:
                            b.wait("tensor", tq_done[6])
                        if tq_done[7] is not None:
                            b.wait("tensor", tq_done[7])
                        for sub in range(2):
                            b.do("tensor", lambda e, u=u, th=th, sub=sub: e.matmul(
                                out=PS[:, 3072 + sub * 512:3072 + (sub + 1) * 512],
                                lhsT=esel[:, u * 128:(u + 1) * 128],
                                rhs=combfm_b[:, th * 1024 + sub * 512:
                                             th * 1024 + (sub + 1) * 512],
                                start=True, stop=True))
                        tcbm = b.last("tensor", sT)
                        b.wait("vector", tcbm)
                        tcb = b.inc("vector", lambda e: e.tensor_copy(
                            out=cbe[:, 0:1024], in_=PS[:, 3072:4096]), sV, 1)
                        tq_done[6] = tcb
                        tq_done[7] = tcb
                    b.wait("scalar", hml[0])
                    for hc in range(4):
                        for t2 in range(2):
                            bk = (hc * 2 + t2) % 6
                            if tq_done[bk] is not None:
                                b.wait("tensor", tq_done[bk])
                            for dk in range(DC):
                                b.do("tensor", lambda e, u=u, th=th, hc=hc, t2=t2,
                                     dk=dk, Au=Au, bk=bk: e.matmul(
                                    out=bank(bk),
                                    lhsT=b2s(Au, dk * 512 + hc * 128, 128),
                                    rhs=b1v(dk, th * 1024 + t2 * 512, 512),
                                    start=(dk == 0), stop=(dk == DC - 1)))
                            tmm = b.last("tensor", sT)
                            b.wait("scalar", tmm)
                            tsv = b.inc("scalar", lambda e, hc=hc, t2=t2, bk=bk:
                                        e.activation(
                                out=SV[:, hc * 1024 + t2 * 512: hc * 1024 + t2 * 512 + 512], in_=bank(bk),
                                func=AF.Silu), sA, 1)
                            tq_done[bk] = tsv
                            tsv_map[(hc, t2)] = tsv
                            thin_mm = tmm
                    thin_done = tsv
                    if th == 0:
                        b.wait("tensor", t_b2[Bu])
                    for hc in range(4):
                        for t2 in range(2):
                            bk = (hc * 2 + t2) % 6
                            if tq_done[bk] is not None:
                                b.wait("tensor", tq_done[bk])
                            for dk in range(DC):
                                b.do("tensor", lambda e, u=u, th=th, hc=hc, t2=t2,
                                     dk=dk, Bu=Bu, bk=bk: e.matmul(
                                    out=bank(bk),
                                    lhsT=b2s(Bu, dk * 512 + hc * 128, 128),
                                    rhs=b1v(dk, th * 1024 + t2 * 512, 512),
                                    start=(dk == 0), stop=(dk == DC - 1)))
                            tmm = b.last("tensor", sT)
                            b.wait("vector", tmm, hml[0])
                            if gated:
                                b.wait("vector", tcb)
                                tvc = b.inc("vector", lambda e, t2=t2, bk=bk:
                                            e.tensor_tensor(
                                    out=VCX, in0=bank(bk),
                                    in1=cbe[:, t2 * 512:(t2 + 1) * 512],
                                    op=ALU.mult), sV, 1)
                            else:
                                tvc = b.inc("vector", lambda e, bk=bk: e.tensor_copy(
                                    out=VCX, in_=bank(bk)), sV, 1)
                            tq_done[bk] = tvc
                            ssl = (hc * 2 + t2) % 2
                            b.wait("vector", tvc, tsv_map[(hc, t2)])
                            if stg_done[ssl] is not None:
                                b.wait("vector", stg_done[ssl])
                            hml[0] = th_ = b.inc("vector", lambda e, hc=hc, t2=t2, ssl=ssl:
                                        e.tensor_tensor(
                                out=stage[:, ssl * 512:(ssl + 1) * 512],
                                in0=SV[:, hc * 1024 + t2 * 512: hc * 1024 + t2 * 512 + 512], in1=VCX,
                                op=ALU.mult), sV, 1)
                            b.wait("gpsimd", th_)
                            tdma = b.dma("gpsimd",
                                         h_sc[u * H + hc * 128: u * H + (hc + 1) * 128,
                                              th * 1024 + t2 * 512:
                                              th * 1024 + (t2 + 1) * 512],
                                         stage[:, ssl * 512:(ssl + 1) * 512], sHo[ssl])
                            stg_done[ssl] = tdma
                    if th == 1 and u + 1 < NU:
                        t_b2[Au] = load_unit_w(u + 1, 1, Au, tgate=thin_mm)
                if first_call:
                    cast_wout(u)
                tunit_done = (sT, sT.n)
            th_all = [(s_, s_.n) for s_ in sHo]
            two_all = (sWoD, sWoD.n)

            wsl_tok = {0: None, 1: None}
            wsl_used = {0: None, 1: None}

            def load_wout_b2(u, sl):
                if wsl_used[sl] is not None:
                    b.wait("sync", wsl_used[sl])
                if first_call:
                    b.wait("sync", two_all)
                dv = wout_sc[u * H:(u + 1) * H, :].rearrange("(r p) m -> p r m", p=128)
                wsl_tok[sl] = b.dma("sync", B2[:, sl * 4096:(sl + 1) * 4096]
                                    .rearrange("p (r m) -> p r m", m=1024), dv,
                                    sWA if sl == 0 else sWB)

            tev_tt = None
            wsl_used[0] = tunit_done
            wsl_used[1] = tunit_done
            hs_tok = [hml[0]] * 8
            hsv = lambda j: SV[:, j * 512:(j + 1) * 512]
            for tt in range(4):
                load_wout_b2(0, 0)
                first_mm = True
                items = [(u, hc) for u in range(NU) for hc in range(4)]

                def hload(i, tt=tt):
                    u, hc = items[i]
                    hsl = i % 8
                    if hs_tok[hsl] is not None:
                        b.wait("sync", hs_tok[hsl])
                    b.wait("sync", *th_all)
                    return b.dma("sync", hsv(hsl),
                                 h_sc[u * H + hc * 128: u * H + (hc + 1) * 128,
                                      tt * 512:(tt + 1) * 512], sHsl[hsl])
                tldhs = [hload(j) for j in range(6)]
                for i, (u, hc) in enumerate(items):
                    sl = u % 2
                    if hc == 0 and u + 1 < NU:
                        load_wout_b2(u + 1, 1 - sl)
                    if i + 6 < len(items):
                        tldhs.append(hload(i + 6))
                    hsl = i % 8
                    b.wait("tensor", wsl_tok[sl], tldhs[i])
                    for dc in range(DC):
                        b.do("tensor", lambda e, u=u, hc=hc, dc=dc, sl=sl,
                             hsl=hsl, fm=first_mm: e.matmul(
                            out=bank(dc),
                            lhsT=B2[:, sl * 4096 + hc * 1024 + dc * 128:
                                    sl * 4096 + hc * 1024 + (dc + 1) * 128],
                            rhs=hsv(hsl),
                            start=fm, stop=(u == NU - 1 and hc == 3)))
                    first_mm = False
                    thc = b.last("tensor", sT)
                    hs_tok[hsl] = thc
                    if hc == 3:
                        wsl_used[sl] = thc
                tmm_done = (sT, sT.n)
                b.wait("vector", tmm_done)
                for dc in range(DC):
                    b.do("vector", lambda e, tt=tt, dc=dc: e.tensor_tensor(
                        out=xv(dc, tt * 512, 512), in0=bank(dc),
                        in1=xv(dc, tt * 512, 512), op=ALU.add))
                tev_tt = b.last("vector", sV)
                b.wait("tensor", tev_tt)
                wsl_used[0] = tev_tt
                wsl_used[1] = tev_tt
            return tev_tt


        sVt = [Sem(es, nc, f"sVt{i}") for i in range(2)]
        sKpo = Sem(es, nc, "sKpo")
        sKp = Sem(es, nc, "sKp")
        sVs = [Sem(es, nc, f"sVs{i}") for i in range(2)]
        sZo = [Sem(es, nc, f"sZo{i}") for i in range(2)]
        sZi = Sem(es, nc, "sZi")
        sH1 = [Sem(es, nc, f"sH1{i}") for i in range(2)]
        sF2 = [Sem(es, nc, f"sF2{i}") for i in range(4)]
        sAR2 = Sem(es, nc, "sAR2")

        def emit_s3(tprev):
            def post_rows(tq, tm2):
                s0 = tq * 512
                b.wait("vector", tm2)
                t_ = b.inc("vector", lambda e: e.tensor_tensor(out=ROW2, in0=VAR,
                                                               in1=VAR, op=ALU.mult), sV, 1)
                b.wait("vector", t_)
                t_ = b.inc("vector", lambda e: e.tensor_scalar(
                    out=ROW2, in0=ROW2, scalar1=-LN_EPS, scalar2=1.0,
                    op0=ALU.mult, op1=ALU.add), sV, 1)
                b.wait("scalar", t_)
                t_ = b.inc("scalar", lambda e: e.activation(out=ROW2, in_=ROW2,
                                                            func=AF.Ln, bias=LN_EPS), sA, 1)
                b.wait("scalar", t_)
                t_ = b.inc("scalar", lambda e: e.activation(out=ROW2, in_=ROW2,
                                                            func=AF.Exp, scale=-0.5), sA, 1)
                b.wait("vector", t_)
                t_ = b.inc("vector", lambda e: e.tensor_tensor(out=MU, in0=VAR,
                                                               in1=ROW2, op=ALU.mult), sV, 1)
                b.wait("vector", t_)
                tm = b.inc("vector", lambda e: e.tensor_tensor(out=ROW2, in0=TROW,
                                                               in1=ROW2, op=ALU.mult), sV, 1)
                b.wait("tensor", tm)
                b.do("tensor", lambda e, s0=s0: e.matmul(
                    out=PS[:, s0:s0 + 512], lhsT=ones_f[:],
                    rhs=MU, start=True, stop=True))
                b.do("tensor", lambda e, s0=s0: e.matmul(
                    out=PS[:, 2048 + s0:2048 + s0 + 512], lhsT=ones_f[:],
                    rhs=ROW2, start=True, stop=True))
                tb2_ = b.last("tensor", sT)
                b.wait("vector", tb2_)
                b.do("vector", lambda e, s0=s0: e.tensor_copy(
                    out=SB_[:, s0:s0 + 512], in_=PS[:, s0:s0 + 512]))
                tz = b.inc("vector", lambda e, s0=s0: e.tensor_copy(
                    out=M2B[:, s0:s0 + 512], in_=PS[:, 2048 + s0:2048 + s0 + 512]),
                    sV, 1)
                b.wait("tensor", tz)
                return VAR, TROW, tz

            tbp = emit_ln_stats_rolled(lambda dc: xv(dc, 0, TC), after=[tprev],
                                       sb_dst=SBP, m2_dst=M2BP, post_rows=post_rows)
            ta1 = emit_norm(lambda dc: xv(dc, 0, TC), lambda dc: b1v(dc, 0, TC),
                            lnw=True, lnwoff=VEC["b_ln1w"], after=[tbp])

            tqw = stage_load(b_qT.rearrange("(dk p) m -> p dk m", p=128),
                             stage[:, 0:512], tgate=tprev)
            tkw = stage_load(b_kT.rearrange("(dk p) m -> p dk m", p=128),
                             stage[:, 512:1024], tgate=tprev)
            qw_t = lambda dk: stage[:, dk * S:(dk + 1) * S]
            kw_t = lambda dk: stage[:, 512 + dk * S:512 + (dk + 1) * S]
            b.wait("tensor", tqw, tkw, ta1)
            for tq in range(4):
                for dk in range(DC):
                    b.do("tensor", lambda e, tq=tq, dk=dk: e.matmul(
                        out=PS[0:S, tq * 512:(tq + 1) * 512], lhsT=qw_t(dk),
                        rhs=b1v(dk, tq * 512, 512), start=(dk == 0), stop=(dk == DC - 1)))
            for tq in range(4):
                for dk in range(DC):
                    b.do("tensor", lambda e, tq=tq, dk=dk: e.matmul(
                        out=PS[0:S, 2048 + tq * 512:2048 + (tq + 1) * 512], lhsT=kw_t(dk),
                        rhs=b1v(dk, tq * 512, 512), start=(dk == 0), stop=(dk == DC - 1)))
            tqk = b.last("tensor", sT)

            def elu1(psoff, bias_col, dst_bf):
                b.wait("vector", tqk)
                t1 = b.inc("vector", lambda e, psoff=psoff, bias_col=bias_col:
                           e.tensor_scalar(
                    out=ksqk[:], in0=PS[0:S, psoff:psoff + TC], scalar1=bias_col,
                    scalar2=0.0, op0=ALU.add, op1=ALU.max), sV, 1)
                t2 = b.inc("vector", lambda e, psoff=psoff, bias_col=bias_col:
                           e.tensor_scalar(
                    out=SC[0:S, 0:TC], in0=PS[0:S, psoff:psoff + TC], scalar1=bias_col,
                    scalar2=0.0, op0=ALU.add, op1=ALU.min), sV, 1)
                b.wait("scalar", t2)
                t3 = b.inc("scalar", lambda e: e.activation(out=SC[0:S, 0:TC],
                                                            in_=SC[0:S, 0:TC],
                                                            func=AF.Exp), sA, 1)
                b.wait("vector", t3, t1)
                t4 = b.inc("vector", lambda e, dst_bf=dst_bf: e.tensor_tensor(
                    out=dst_bf, in0=SC[0:S, 0:TC], in1=ksqk[:], op=ALU.add), sV, 1)
                return t4

            tQ = elu1(0, v64c("b_qb"), qb16[:])
            b.wait("vector", tQ)
            tK = elu1(2048, v64c("b_kb"), kb16[:])
            b.wait("vector", tK)
            t_ = b.inc("vector", lambda e: e.memset(wful_t[0:S, 0:SEG], 1.0), sV, 1)
            b.wait("vector", t_)
            tks = emit_scan(lambda c0, n: ksqk[:, c0:c0 + n], wful_t[0:S, 0:SEG],
                            lambda c0, n: kb16[:, c0:c0 + n], TC)
            b.wait("vector", tks)
            tkst = b.inc("vector", lambda e: e.tensor_copy(out=kvrun[:, 1024:1025],
                                                           in_=ksqk[:, TC - 1:TC]), sV, 1)
            b.wait("vector", tkst)
            tqks = b.inc("vector", lambda e: e.tensor_tensor(out=ksqk[:], in0=qb16[:],
                                                             in1=ksqk[:], op=ALU.mult),
                         sV, 1)
            b.wait("tensor", tqks)
            for c in range(NCH):
                b.do("tensor", lambda e, c=c: e.matmul(
                    out=PS[:, 1024 + c:1025 + c], lhsT=ksqk[:, c * CH:(c + 1) * CH],
                    rhs=onesc_b[0:S, :], start=True, stop=True))
            tden = b.last("tensor", sT)
            b.wait("vector", tden)
            tdenl = b.inc("vector", lambda e: e.tensor_scalar(
                out=denloc[:], in0=PS[:, 1024:1040], scalar1=ATTN_EPS, scalar2=None,
                op0=ALU.add), sV, 1)

            st_state["cons"][0] = tK
            st_state["cons"][1] = tK
            v3 = b_vT.rearrange("(dk p) m -> p dk m", p=128)
            tvw = None
            for pc in range(8):
                tvw = stage_load(v3[:, pc:pc + 1, :], B2[:, pc * 1024:(pc + 1) * 1024],
                                 tgate=tprev)
            tvb = stage_load(b_vbrow, vbrow_b[0:1, :], tgate=tprev)
            vw_t = lambda dk, dh: B2[:, dk * 1024 + dh * 512: dk * 1024 + (dh + 1) * 512]

            f2c_state = {"woc": (sWoD, sWoD.n)}

            def emit_f2cast_piece(pc):
                i = st_state["i"] % 2
                st_state["i"] += 1
                slot = SC[:, i * 1024:i * 1024 + 1024]
                if st_state["cons"][i] is not None:
                    b.wait("sync", st_state["cons"][i])
                tld = b.dma("sync", slot,
                            b_f2T[pc * 128:(pc + 1) * 128, :], sStA if i == 0 else sStB)
                b.wait("scalar", tld)
                tcwoc = None
                for hh in range(4):
                    b.wait("scalar", f2c_state["woc"])
                    tcwoc = b.inc("scalar", lambda e, slot=slot, hh=hh: e.activation(
                        out=WOC, in_=slot[:, hh * 256:(hh + 1) * 256], func=AF.Copy), sA, 1)
                    b.wait("gpsimd", tcwoc)
                    f2c_state["woc"] = b.dma(
                        "gpsimd", f2_sc[pc * 128:(pc + 1) * 128,
                                        hh * 256:(hh + 1) * 256], WOC, sWoD)
                st_state["cons"][i] = tcwoc

            b.wait("vector", tkst)
            b.do("vector", lambda e: e.memset(kvrun[:, 1025:1040], 0.0))
            t_kvz = b.inc("vector", lambda e: e.memset(kvrun[:, 0:1024], 0.0), sV, 1)
            tv_ev = [None, None]
            tkv_mm = None
            tktm_prev = None
            tkvadd = None
            b.wait("tensor", tvw, tvb)
            for c in range(NCH):
                for dh in range(2):
                    for dk in range(DC):
                        b.do("tensor", lambda e, c=c, dh=dh, dk=dk: e.matmul(
                            out=PS[:, 2048 + dh * 512:2048 + (dh + 1) * 512],
                            lhsT=b1v(dk, c * CH, CH), rhs=vw_t(dk, dh),
                            start=(dk == 0), stop=False))
                    b.do("tensor", lambda e, c=c, dh=dh: e.matmul(
                        out=PS[:, 2048 + dh * 512:2048 + (dh + 1) * 512],
                        lhsT=ones_b[:, 0:CH], rhs=vbrow_b[0:1, dh * 512:(dh + 1) * 512],
                        start=False, stop=True))
                tvmm = b.last("tensor", sT)
                vsl = c % 2
                b.wait("vector", tvmm, tv_ev[vsl])
                tvev = b.inc("vector", lambda e, vsl=vsl: e.tensor_copy(
                    out=SV[:, vsl * 1024:(vsl + 1) * 1024], in_=PS[:, 2048:3072]), sV, 1)
                b.wait("gpsimd", tvev)
                tvout = b.dma("gpsimd", vtm_sc[c * CH:(c + 1) * CH, :],
                              SV[:, vsl * 1024:(vsl + 1) * 1024], sVt[vsl])
                tv_ev[vsl] = tvout
                b.wait("vector", tktm_prev)
                tkf = b.inc("vector", lambda e, c=c: e.tensor_copy(
                    out=agkv[0:S, 0:CH], in_=kb16[:, c * CH:(c + 1) * CH]), sV, 1)
                b.wait("tensor", tkf, tkv_mm)
                tktr = b.inc("tensor", lambda e: e.transpose(
                    out=PS[0:CH, 1536:1600], in_=agkv[0:S, 0:CH],
                    identity=ident[0:S, 0:S]), sT, 1)
                b.wait("vector", tktr)
                tktm = b.inc("vector", lambda e: e.tensor_copy(out=ktm_b[:],
                                                               in_=PS[0:CH, 1536:1600]),
                             sV, 1)
                tktm_prev = tktm
                b.wait("vector", t_kvz if c == 0 else tkvadd)
                tkpc = b.inc("vector", lambda e: e.tensor_copy(out=kvpref_b[:],
                                                               in_=kvrun[:, 0:1024]),
                             sV, 1)
                b.wait("gpsimd", tkpc)
                tkpo = b.dma("gpsimd", kvpref_sc[c * S:(c + 1) * S, :], kvpref_b[:],
                             sKpo)
                b.wait("tensor", tktm, tvev)
                for dh in range(2):
                    b.do("tensor", lambda e, dh=dh, vsl=vsl: e.matmul(
                        out=PS[0:S, 3072 + dh * 512:3072 + (dh + 1) * 512],
                        lhsT=ktm_b[:], rhs=SV[:, vsl * 1024 + dh * 512:
                                              vsl * 1024 + (dh + 1) * 512],
                        start=True, stop=True))
                tkv_mm = b.last("tensor", sT)
                b.wait("vector", tkv_mm, tkpo)
                tkvadd = b.inc("vector", lambda e: e.tensor_tensor(
                    out=kvrun[:, 0:1024], in0=kvrun[:, 0:1024],
                    in1=PS[0:S, 3072:4096], op=ALU.add), sV, 1)
                emit_f2cast_piece(2 * c)
                emit_f2cast_piece(2 * c + 1)
            tpassA = tkvadd
            tf2w_all = (sWoD, sWoD.n)

            b.wait("vector", tpassA, tkst)
            tpd = None
            for r in range(NCORE):
                for pp in range(2):
                    t_ = b.inc("vector", lambda e, r=r, pp=pp: e.tensor_scalar(
                        out=agkv[:, 0:520], in0=kvrun[:, pp * 520:(pp + 1) * 520],
                        scalar1=v64[:, V64["ohc"] + r:V64["ohc"] + r + 1], scalar2=None,
                        op0=ALU.mult), sV, 1)
                    b.wait("gpsimd", t_)
                    tpd = b.dma("gpsimd", ag2_in[:, r * 1040 + pp * 520:
                                                 r * 1040 + (pp + 1) * 520],
                                agkv[:, 0:520], sAR2)
                    b.wait("vector", tpd)
            b.wait("gpsimd", tpd)
            tcc2 = b.inc("gpsimd", lambda e: e.collective_compute(
                "AllReduce", ALU.add, replica_groups=[list(range(NCORE))],
                ins=[ag2_in], outs=[ag2_out]), sC2, 1)
            b.wait("gpsimd", tcc2)
            b.wait("vector", tcc2)
            tacc = b.inc("vector", lambda e: e.memset(kvrun[:], 0.0), sV, 1)
            for r in range(NCORE):
                for pp in range(2):
                    b.wait("gpsimd", tacc)
                    tld = b.dma("gpsimd", agkv[:, 0:520],
                                ag2_out[:, r * 1040 + pp * 520:r * 1040 + (pp + 1) * 520],
                                sAR2)
                    b.wait("vector", tld)
                    tacc = b.inc("vector", lambda e, r=r, pp=pp: e.scalar_tensor_tensor(
                        out=kvrun[:, pp * 520:(pp + 1) * 520], in0=agkv[:, 0:520],
                        scalar=v64[:, V64["kv_mask"] + r:V64["kv_mask"] + r + 1],
                        in1=kvrun[:, pp * 520:(pp + 1) * 520],
                        op0=ALU.mult, op1=ALU.add), sV, 1)
            b.wait("vector", tacc)
            tkvin = b.inc("vector", lambda e: e.tensor_copy(out=kvin_b[:, 0:1040],
                                                            in_=kvrun[:, 0:1040]), sV, 1)


            tnum_ev = None
            tkpin = None
            tkpin_d = None
            tkpo_all = (sKpo, sKpo.n)
            tvt_all = [(s_, s_.n) for s_ in sVt]
            tvs = [None, None]
            b.wait("sync", tkpo_all, *tvt_all)
            for c in range(NCH):
                b.wait("sync", tkpin)
                tkpin_d = b.dma("sync", kvpref_b[:], kvpref_sc[c * S:(c + 1) * S, :], sKp)
                vsl = c % 2
                b.wait("sync", tvs[vsl])
                tvsd = b.dma("sync", SV[:, 2048 + vsl * 1024:2048 + (vsl + 1) * 1024],
                             vtm_sc[c * CH:(c + 1) * CH, :], sVs[vsl])
                b.wait("tensor", tkvin, tdenl)
                tdmm = b.inc("tensor", lambda e, c=c: e.matmul(
                    out=PS[:, 1024:1025], lhsT=qb16[:, c * CH:(c + 1) * CH],
                    rhs=kvin_b[:, 1024:1025], start=True, stop=True), sT, 1)
                b.wait("vector", tdmm, tnum_ev)
                t_ = b.inc("vector", lambda e, c=c: e.tensor_tensor(
                    out=den1[:, 0:1], in0=PS[:, 1024:1025], in1=denloc[:, c:c + 1],
                    op=ALU.add), sV, 1)
                b.wait("vector", t_)
                t_ = b.inc("vector", lambda e: e.reciprocal(out=den1[:, 1:2],
                                                            in_=den1[:, 0:1]), sV, 1)
                b.wait("tensor", t_)
                ttr1 = b.inc("tensor", lambda e: e.transpose(
                    out=PS[0:1, 1152:1280], in_=den1[:, 1:2], identity=ident[:]), sT, 1)
                b.wait("vector", ttr1)
                t_ = b.inc("vector", lambda e: e.tensor_copy(out=rrow[0:1, 0:128],
                                                             in_=PS[0:1, 1152:1280]),
                           sV, 1)
                b.wait("tensor", t_)
                tbc1 = b.inc("tensor", lambda e: e.matmul(
                    out=PS[0:S, 1280:1408], lhsT=ones_f[0:1, 0:S], rhs=rrow[0:1, 0:128],
                    start=True, stop=True), sT, 1)
                b.wait("vector", tbc1)
                t_ = b.inc("vector", lambda e: e.tensor_copy(out=agkv[0:S, 128:256],
                                                             in_=PS[0:S, 1280:1408]),
                           sV, 1)
                b.wait("vector", t_)
                tqp = b.inc("vector", lambda e, c=c: e.tensor_tensor(
                    out=qpb[:], in0=qb16[:, c * CH:(c + 1) * CH],
                    in1=agkv[0:S, 128:256], op=ALU.mult), sV, 1)
                b.wait("tensor", tqp)
                tamm = b.inc("tensor", lambda e, c=c: e.matmul(
                    out=PS[:, 1408:1536], lhsT=kb16[:, c * CH:(c + 1) * CH], rhs=qpb[:],
                    start=True, stop=True), sT, 1)
                b.wait("vector", tamm)
                tat = b.inc("vector", lambda e: e.tensor_tensor(
                    out=atile[:], in0=PS[:, 1408:1536], in1=tri[:], op=ALU.mult), sV, 1)
                b.wait("tensor", tkpin_d, tat, tvsd)
                for dh2 in range(DC):
                    b.do("tensor", lambda e, dh2=dh2: e.matmul(
                        out=PS[:, dh2 * 128:(dh2 + 1) * 128],
                        lhsT=kvpref_b[:, dh2 * 128:(dh2 + 1) * 128], rhs=qpb[:],
                        start=True, stop=False))
                    b.do("tensor", lambda e, dh2=dh2: e.matmul(
                        out=PS[:, dh2 * 128:(dh2 + 1) * 128],
                        lhsT=kvin_b[:, dh2 * 128:(dh2 + 1) * 128], rhs=qpb[:],
                        start=False, stop=False))
                    b.do("tensor", lambda e, dh2=dh2, vsl=vsl: e.matmul(
                        out=PS[:, dh2 * 128:(dh2 + 1) * 128],
                        lhsT=SV[:, 2048 + vsl * 1024 + dh2 * 128:
                                2048 + vsl * 1024 + (dh2 + 1) * 128],
                        rhs=atile[:], start=False, stop=True))
                tnmm = b.last("tensor", sT)
                tkpin = tnmm
                tvs[vsl] = tnmm
                b.wait("vector", tnmm)
                tnum_ev = b.inc("vector", lambda e, c=c: e.tensor_copy(
                    out=B1[:].rearrange("p (dc t) -> p dc t", t=TC)
                    [:, :, c * CH:(c + 1) * CH],
                    in_=PS[:, 0:1024].rearrange("p (dc i) -> p dc i", i=CH)), sV, 1)
                b.wait("tensor", tnum_ev)
            tpassB = tnum_ev

            st_state["cons"][0] = tpassB
            st_state["cons"][1] = tpassB
            o3 = b_oT.rearrange("(dk p) m -> p dk m", p=128)
            tow = None
            for pc in range(8):
                tow = stage_load(o3[:, pc:pc + 1, :], B2[:, pc * 1024:(pc + 1) * 1024],
                                 tgate=tpassA)
            ow_t = lambda dk, dc: B2[:, dk * 1024 + dc * 128: dk * 1024 + (dc + 1) * 128]
            tzo = [None, None]
            tox = None
            b.wait("tensor", tow, tpassB)
            for tq in range(4):
                for dc in range(DC):
                    for dk in range(DC):
                        b.do("tensor", lambda e, tq=tq, dc=dc, dk=dk: e.matmul(
                            out=bank(dc), lhsT=ow_t(dk, dc),
                            rhs=b1v(dk, tq * 512, 512),
                            start=(dk == 0), stop=(dk == DC - 1)))
                tomm = b.last("tensor", sT)
                b.wait("vector", tomm, tox)
                for dc in range(DC):
                    zsl = dc % 2
                    t_ = b.inc("vector", lambda e, tq=tq, dc=dc: e.tensor_tensor(
                        out=SC[:, 0:512], in0=xv(dc, tq * 512, 512),
                        in1=SBP[:, tq * 512:(tq + 1) * 512], op=ALU.mult), sV, 1)
                    b.wait("vector", t_)
                    t_ = b.inc("vector", lambda e, tq=tq: e.tensor_tensor(
                        out=SC[:, 0:512], in0=SC[:, 0:512],
                        in1=M2BP[:, tq * 512:(tq + 1) * 512], op=ALU.subtract), sV, 1)
                    b.wait("vector", t_, tzo[zsl])
                    tz_ = b.inc("vector", lambda e, dc=dc, zsl=zsl: e.scalar_tensor_tensor(
                        out=stage[:, zsl * 512:(zsl + 1) * 512], in0=bank(dc),
                        scalar=vec8[:, VEC["b_ob"] + dc:VEC["b_ob"] + dc + 1],
                        in1=SC[:, 0:512], op0=ALU.add, op1=ALU.add), sV, 1)
                    b.wait("gpsimd", tz_)
                    tzd = b.dma("gpsimd", z_sc.rearrange("(dc p) t -> p dc t", p=128)
                                [:, dc:dc + 1, tq * 512:(tq + 1) * 512],
                                stage[:, zsl * 512:(zsl + 1) * 512], sZo[zsl])
                    tzo[zsl] = tzd
                    t_ = b.inc("vector", lambda e, tq=tq, dc=dc: e.tensor_tensor(
                        out=xv(dc, tq * 512, 512), in0=xv(dc, tq * 512, 512),
                        in1=SC[:, 0:512], op=ALU.add), sV, 1)
                    b.wait("vector", t_)
                    tox = b.inc("vector", lambda e, tq=tq, dc=dc: e.scalar_tensor_tensor(
                        out=xv(dc, tq * 512, 512), in0=bank(dc),
                        scalar=vec8[:, VEC["b_ob"] + dc:VEC["b_ob"] + dc + 1],
                        in1=xv(dc, tq * 512, 512), op0=ALU.add, op1=ALU.add), sV, 1)
                b.wait("tensor", tox)
            tzo_all = [(s_, s_.n) for s_ in sZo]

            b.wait("scalar", tf2w_all)
            tprevz = None
            for dc in range(DC):
                if tprevz is not None:
                    b.wait("sync", tprevz)
                else:
                    b.wait("sync", tox, *tzo_all)
                tldz = b.dma("sync", SCB[:, 0:2048],
                             z_sc[dc * 128:(dc + 1) * 128, :], sZi)
                b.wait("scalar", tldz)
                tsq = b.inc("scalar", lambda e: e.activation(out=SQR, in_=SCB[:, 0:2048],
                                                             func=AF.Square), sA, 1)
                b.wait("tensor", tldz)
                for tq in range(4):
                    b.do("tensor", lambda e, tq=tq, dc=dc: e.matmul(
                        out=PS[0:1, tq * 512:(tq + 1) * 512], lhsT=onesc_b[:],
                        rhs=SCB[:, tq * 512:(tq + 1) * 512],
                        start=(dc == 0), stop=(dc == DC - 1)))
                b.wait("tensor", tsq)
                for tq in range(4):
                    b.do("tensor", lambda e, tq=tq, dc=dc: e.matmul(
                        out=PS[0:1, 2048 + tq * 512:2048 + (tq + 1) * 512],
                        lhsT=onesc_b[:], rhs=SQR[:, tq * 512:(tq + 1) * 512],
                        start=(dc == 0), stop=(dc == DC - 1)))
                tprevz = b.last("tensor", sT)
                b.wait("scalar", tprevz)
            tbz = emit_rows_bcast(tprevz, 2048)
            tnz = None
            tzi2 = None
            b.wait("sync", tprevz)
            for dc in range(DC):
                for tp in range(4):
                    zsl = tp % 2
                    b.wait("sync", tnz, tzi2)
                    tldz = b.dma("sync", stage[:, zsl * 512:(zsl + 1) * 512],
                                 z_sc[dc * 128:(dc + 1) * 128, tp * 512:(tp + 1) * 512],
                                 sZi)
                    tzi2 = tldz
                    b.wait("vector", tldz, tbz)
                    t_ = b.inc("vector", lambda e, dc=dc, tp=tp, zsl=zsl: e.tensor_tensor(
                        out=b1v(dc, tp * 512, 512),
                        in0=stage[:, zsl * 512:(zsl + 1) * 512],
                        in1=SB_[:, tp * 512:(tp + 1) * 512], op=ALU.mult), sV, 1)
                    b.wait("vector", t_)
                    t_ = b.inc("vector", lambda e, dc=dc, tp=tp: e.tensor_tensor(
                        out=b1v(dc, tp * 512, 512), in0=b1v(dc, tp * 512, 512),
                        in1=M2B[:, tp * 512:(tp + 1) * 512], op=ALU.subtract), sV, 1)
                    b.wait("vector", t_)
                    tnz = b.inc("vector", lambda e, dc=dc, tp=tp: e.tensor_scalar(
                        out=b1v(dc, tp * 512, 512), in0=b1v(dc, tp * 512, 512),
                        scalar1=vec8[:, VEC["b_ln2w"] + dc:VEC["b_ln2w"] + dc + 1],
                        scalar2=vec8[:, VEC["b_ln2b"] + dc:VEC["b_ln2b"] + dc + 1],
                        op0=ALU.mult, op1=ALU.add), sV, 1)
            ta2 = tnz

            st_state["cons"][0] = ta2
            st_state["cons"][1] = ta2
            th1o = [None, None]
            tf1_used = None
            tbk = [None] * 8
            for hq in range(4):
                f13 = b_f1T[:, hq * 1024:(hq + 1) * 1024].rearrange(
                    "(dk p) m -> p dk m", p=128)
                tf1 = None
                for pc in range(8):
                    tf1 = stage_load(f13[:, pc:pc + 1, :],
                                     B2[:, pc * 1024:(pc + 1) * 1024],
                                     tgate=tf1_used if hq > 0 else ta2)
                f1_t = lambda dk, hr: B2[:, dk * 1024 + hr * 128: dk * 1024 + (hr + 1) * 128]
                b.wait("tensor", tf1, ta2)
                for hr in range(8):
                    for tq in range(4):
                        bk = (hr * 4 + tq) % 8
                        if tbk[bk] is not None:
                            b.wait("tensor", tbk[bk])
                        for dk in range(DC):
                            b.do("tensor", lambda e, hq=hq, hr=hr, tq=tq, dk=dk, bk=bk:
                                 e.matmul(
                                out=bank(bk), lhsT=f1_t(dk, hr),
                                rhs=b1v(dk, tq * 512, 512),
                                start=(dk == 0), stop=(dk == DC - 1)))
                        tmm = b.last("tensor", sT)
                        hsl = (hr * 4 + tq) % 2
                        b.wait("scalar", tmm, th1o[hsl])
                        tg_ = b.inc("scalar", lambda e, hq=hq, hr=hr, bk=bk, hsl=hsl:
                                    e.activation(
                            out=stage[:, hsl * 512:(hsl + 1) * 512], in_=bank(bk),
                            func=AF.Gelu,
                            bias=f1b_sb[:, hq * 8 + hr:hq * 8 + hr + 1]), sA, 1)
                        tbk[bk] = tg_
                        b.wait("gpsimd", tg_)
                        th1d = b.dma("gpsimd",
                                     h1_sc[hq * 1024 + hr * 128:
                                           hq * 1024 + (hr + 1) * 128,
                                           tq * 512:(tq + 1) * 512],
                                     stage[:, hsl * 512:(hsl + 1) * 512], sH1[hsl])
                        th1o[hsl] = th1d
                tf1_used = (sT, sT.n)
            th1_all = [(s_, s_.n) for s_ in sH1]

            tf2sl = [tf1_used] * 4
            thsl2 = [None] * 8
            tevx = None
            hsv = lambda j: SV[:, j * 512:(j + 1) * 512]
            for tt in range(4):
                first_mm = True

                def f2load(kq, tt=tt):
                    fsl = kq % 4
                    b.wait("sync", tf2sl[fsl], tf2w_all)
                    tfld = b.dma("sync", B2[:, fsl * 1024:(fsl + 1) * 1024],
                                 f2_sc[kq * 128:(kq + 1) * 128, :], sF2[fsl])
                    hsl = kq % 8
                    b.wait("sync", thsl2[hsl], *th1_all)
                    tldh = b.dma("sync", hsv(hsl),
                                 h1_sc[kq * 128:(kq + 1) * 128,
                                       tt * 512:(tt + 1) * 512], sHsl[hsl])
                    return (tfld, tldh)
                lds = [f2load(j) for j in range(3)]
                for kq in range(32):
                    if kq + 3 < 32:
                        lds.append(f2load(kq + 3))
                    fsl = kq % 4
                    hsl = kq % 8
                    b.wait("tensor", lds[kq][0], lds[kq][1], tevx)
                    for dc in range(DC):
                        b.do("tensor", lambda e, kq=kq, dc=dc, fsl=fsl, hsl=hsl,
                             fm=first_mm: e.matmul(
                            out=bank(dc),
                            lhsT=B2[:, fsl * 1024 + dc * 128:
                                    fsl * 1024 + (dc + 1) * 128],
                            rhs=hsv(hsl),
                            start=fm, stop=(kq == 31)))
                    first_mm = False
                    tmm = b.last("tensor", sT)
                    tf2sl[fsl] = tmm
                    thsl2[hsl] = tmm
                tmm_done = (sT, sT.n)
                b.wait("vector", tmm_done)
                for dc in range(DC):
                    b.do("vector", lambda e, tt=tt, dc=dc: e.scalar_tensor_tensor(
                        out=xv(dc, tt * 512, 512), in0=bank(dc),
                        scalar=vec8[:, VEC["b_f2b"] + dc:VEC["b_f2b"] + dc + 1],
                        in1=xv(dc, tt * 512, 512), op0=ALU.add, op1=ALU.add))
                tevx = b.last("vector", sV)
                b.wait("tensor", tevx)
            return tevx

        tchain = ts1
        if n_sub >= 2:
            tchain = emit_ffn(tchain, first_call=True)
        if n_sub >= 3:
            tchain = emit_s3(tchain)
        if n_sub >= 4:
            tchain = emit_ffn(tchain, first_call=False)
        return finish(nc, es, b, tchain)


def _prep_inputs(inputs):
    x = np.asarray(inputs["L0_attn_a_x"])
    Bb, T, _ = x.shape
    f32 = np.float32
    import ml_dtypes
    bf = ml_dtypes.bfloat16
    t = lambda a: np.ascontiguousarray(np.asarray(a, dtype=f32))
    r8 = lambda v: np.ascontiguousarray(np.asarray(v, f32).reshape(DC, 128).T)

    aw = {}
    aw["a_wshT"] = t(inputs["L0_attn_a_W_shift"].T)
    aw["a_wkT"] = t(inputs["L0_attn_a_W_key"].T)
    aw["a_wvT"] = t(inputs["L0_attn_a_W_value"].T)
    aw["a_woT"] = t(inputs["L0_attn_a_W_output"].T)
    aw["a_gate"] = r8(inputs["L0_attn_a_shift_gate"])
    aw["a_lnw"] = r8(inputs["L0_attn_a_ln_w"])
    aw["a_lnb"] = r8(inputs["L0_attn_a_ln_b"])
    aw["a_td"] = t(np.asarray(inputs["L0_attn_a_time_decay"]).reshape(S, 1))
    aw["a_tf"] = t(np.asarray(inputs["L0_attn_a_time_first"]).reshape(S, 1))
    aw["f_gT"] = t(inputs["L0_ffn_gate_w"].T)
    aw["f_wsin"] = t(inputs["L0_ffn_ws_in"])
    aw["f_vsin"] = t(inputs["L0_ffn_vs_in"])
    aw["f_wsout"] = t(inputs["L0_ffn_ws_out"])
    aw["f_ewin"] = t(np.asarray(inputs["L0_ffn_e_win"]).reshape(E * D, H))
    aw["f_evin"] = t(np.asarray(inputs["L0_ffn_e_vin"]).reshape(E * D, H))
    aw["f_ewout"] = t(np.asarray(inputs["L0_ffn_e_wout"]).reshape(E * H, D))
    aw["b_ln1w"] = r8(inputs["L0_attn_b_ln1_w"]); aw["b_ln1b"] = r8(inputs["L0_attn_b_ln1_b"])
    aw["b_ln2w"] = r8(inputs["L0_attn_b_ln2_w"]); aw["b_ln2b"] = r8(inputs["L0_attn_b_ln2_b"])
    aw["b_qT"] = t(inputs["L0_attn_b_q_w"].T)
    aw["b_qb"] = t(np.asarray(inputs["L0_attn_b_q_b"]).reshape(S, 1))
    aw["b_kT"] = t(inputs["L0_attn_b_k_w"].T)
    aw["b_kb"] = t(np.asarray(inputs["L0_attn_b_k_b"]).reshape(S, 1))
    aw["b_vT"] = t(inputs["L0_attn_b_v_w"].T)
    aw["b_vbrow"] = t(np.asarray(inputs["L0_attn_b_v_b"]).reshape(1, D))
    aw["b_oT"] = t(inputs["L0_attn_b_o_w"].T)
    aw["b_ob"] = r8(inputs["L0_attn_b_o_b"])
    aw["b_f1T"] = t(inputs["L0_attn_b_f1_w"].T)
    aw["b_f1b"] = np.ascontiguousarray(np.asarray(inputs["L0_attn_b_f1_b"], f32).reshape(32, 128).T)
    aw["b_f2T"] = t(inputs["L0_attn_b_f2_w"].T)
    aw["b_f2b"] = r8(inputs["L0_attn_b_f2_b"])
    aw["ident_in"] = np.eye(128, dtype=f32)
    aw["identb_in"] = np.eye(128, dtype=f32).astype(bf)
    trm = np.zeros((128, 128), dtype=f32)
    for j in range(128):
        trm[j, j:] = 1.0
    aw["tri_in"] = trm.astype(bf)
    aw["ones_f_in"] = np.ones((1, 128), f32)
    aw["ones_b_in"] = np.ones((1, 128), bf)
    aw["onesc_b_in"] = np.ones((128, 1), bf)
    aw["consts_in"] = np.broadcast_to(
        np.array([LN_EPS, ATTN_EPS, -0.5, -1.0, 1.0, 0.0, 0.5, 2.0], f32), (128, 8)).copy()
    ese = np.zeros((E, E * 128), bf)
    for e in range(E):
        ese[e, e * 128:(e + 1) * 128] = 1.0
    aw["esel_in"] = ese

    shift = T // 2
    per = []
    for ci in range(NCORE):
        bb, j = ci // 4, ci % 4
        t0 = j * TC
        own = np.ascontiguousarray(x[bb, t0:t0 + TC, :].T.astype(f32))
        s0 = (t0 + shift) % T
        sh = np.ascontiguousarray(x[bb, s0:s0 + TC, :].T.astype(f32))
        base = np.zeros((S, NCORE), f32)
        mask = np.zeros((S, NCORE), f32)
        kvm = np.zeros((S, NCORE), f32)
        for k in range(NCORE):
            kb_, kj = k // 4, k % 4
            if kb_ == bb and kj < j:
                base[:, k] = float(TC) * (j - 1 - kj)
                mask[:, k] = 1.0
                kvm[:, k] = 1.0
        ohc = np.zeros((S, NCORE), f32)
        ohc[:, ci] = 1.0
        im = dict(aw)
        im["ohc_in"] = ohc
        im["xin"] = own
        im["xsh_in"] = sh
        im["rwkv_base"] = base
        im["rwkv_mask"] = mask
        im["kv_maskin"] = kvm
        per.append(im)
    return per


def kernel(**inputs):
    per = _prep_inputs(inputs)
    nc = build_nc(_NSUB)
    res = run_bass_kernel_spmd(nc, per, core_ids=list(range(NCORE)))
    x = np.asarray(inputs["L0_attn_a_x"])
    Bb, T, Dm = x.shape
    out = np.empty((Bb, T, Dm), dtype=np.float32)
    for ci in range(NCORE):
        bb, j = ci // 4, ci % 4
        out[bb, j * TC:(j + 1) * TC, :] = res.results[ci]["xout"].T
    return out
